# version 05
# baseline (speedup 1.0000x reference)
import contextlib
import numpy as np

import concourse.bass as bass
import concourse.mybir as mybir
from concourse.bass_utils import run_bass_kernel_spmd
from concourse.alu_op_type import AluOpType as OP

F32 = mybir.dt.float32
BF16 = mybir.dt.bfloat16
AF = mybir.ActivationFunctionType
AX = mybir.AxisListType

D = 2048
E = 4096
HF = 8192
NS = 16
DC = 64
DTR = 128
RSQN = float(4096.0 ** -0.5)
RSQH = float(8192.0 ** -0.5)


class Sem:
    def __init__(self, h):
        self.h = h
        self.v = 0

    def inc(self, inst, k=1):
        inst.then_inc(self.h, k)
        self.v += k
        return inst


def build(L=2048, LCA=256, LCF=256, stop_stage=4, dbg=False):
    nc = bass.Bass("TRN2", target_bir_lowering=False,
                   detect_race_conditions=False)

    x_in = nc.dram_tensor("x", [L, D], F32, kind="ExternalInput").ap()
    out = nc.dram_tensor("out", [L, D], F32, kind="ExternalOutput").ap()

    def win(name, shape):
        return nc.dram_tensor(name, shape, F32, kind="ExternalInput").ap()

    W_in = win("a_W_in", [2 * E, D])
    conv_w = win("a_conv_w", [E, 4])
    conv_b = win("a_conv_b", [E])
    W_x = win("a_W_x", [DTR + 2 * NS, E])
    W_dt = win("a_W_dt", [E, DTR])
    dt_bias = win("a_dt_bias", [E])
    A_t = win("a_A", [E, NS])
    D_t = win("a_D", [E])
    W_out = win("a_W_out", [D, E])
    Wk = win("b_Wk", [DC, D])
    Wv = win("b_Wv", [DC, D])
    Wq = win("b_Wq", [DC, D])
    Wa_w = win("b_Wa_w", [DC, D])
    Wa_b = win("b_Wa_b", [DC])
    lam = win("b_lam", [DC])
    norm_w = win("b_norm_w", [DC])
    Wo = win("b_Wo", [D, DC])
    W1 = win("f_W1", [HF, D])
    W2 = win("f_W2", [D, HF])
    W3 = win("f_W3", [HF, D])

    ident = nc.dram_tensor("c_ident", [128, 128], F32, kind="ExternalInput").ap()
    h64c = nc.dram_tensor("c_h64", [128, 64], BF16, kind="ExternalInput").ap()
    h128c = nc.dram_tensor("c_h128", [128, 128], BF16, kind="ExternalInput").ap()
    onesc = nc.dram_tensor("c_ones", [128, 128], F32, kind="ExternalInput").ap()

    def scr(name, shape, dt=BF16):
        return nc.dram_tensor(name, shape, dt, kind="Internal").ap()

    WinT = [scr("s_WinT_h", [2 * E // 128, 128, D // 128, 128]),
            scr("s_WinT_l", [2 * E // 128, 128, D // 128, 128])]
    W1T = [scr("s_W1T_h", [HF // 128, 128, D // 128, 128]),
           scr("s_W1T_l", [HF // 128, 128, D // 128, 128])]
    W3T = [scr("s_W3T_h", [HF // 128, 128, D // 128, 128]),
           scr("s_W3T_l", [HF // 128, 128, D // 128, 128])]
    WxT = [scr("s_WxT_h", [E, 160]), scr("s_WxT_l", [E, 160])]
    WdtT = [scr("s_WdtT_h", [DTR, E]), scr("s_WdtT_l", [DTR, E])]
    WofT = [scr("s_WofT_h", [E, D]), scr("s_WofT_l", [E, D])]
    W2fT = [scr("s_W2fT_h", [HF, D]), scr("s_W2fT_l", [HF, D])]
    WbT = {nm: [scr(f"s_{nm}_h", [D, DC]), scr(f"s_{nm}_l", [D, DC])]
           for nm in ("WkT", "WvT", "WqT", "WaT")}
    WoTs = [scr("s_WoT_h", [DC, D]), scr("s_WoT_l", [DC, D])]
    if dbg:
        d_bk = nc.dram_tensor("d_bk", [5 * 64, L], F32,
                              kind="ExternalOutput").ap()
        d_yn = nc.dram_tensor("d_yn", [64, L], F32,
                              kind="ExternalOutput").ap()
        d_lt = nc.dram_tensor("d_lt", [2 * 128, 2048], F32,
                              kind="ExternalOutput").ap()
        d_st = nc.dram_tensor("d_st", [2 * 128, 8], F32,
                              kind="ExternalOutput").ap()
        d_ln = nc.dram_tensor("d_ln", [16 * 128, 256], F32,
                              kind="ExternalOutput").ap()
        d_xq = nc.dram_tensor("d_xq", [32 * 128, 256], F32,
                              kind="ExternalOutput").ap()
        d_u = nc.dram_tensor("d_u", [32 * 128, 256], F32,
                             kind="ExternalOutput").ap()
        d_dt = nc.dram_tensor("d_dt", [32 * 128, 256], F32,
                              kind="ExternalOutput").ap()
        d_y = nc.dram_tensor("d_y", [32 * 128, 256], F32,
                             kind="ExternalOutput").ap()
    X1s = scr("s_X1", [L, D], F32)
    X2s = scr("s_X2", [L, D], F32)
    X3s = scr("s_X3", [L, D], F32)

    ctx = contextlib.ExitStack()

    def sb(name, shape, dt=F32):
        return ctx.enter_context(nc.sbuf_tensor(name, shape, dt))

    sW = Sem(ctx.enter_context(nc.semaphore("sW")))
    sG = Sem(ctx.enter_context(nc.semaphore("sG")))
    sPE = Sem(ctx.enter_context(nc.semaphore("sPE")))
    sV = Sem(ctx.enter_context(nc.semaphore("sV")))
    sAC = Sem(ctx.enter_context(nc.semaphore("sAC")))
    SEMS = [sW, sG, sPE, sV, sAC]

    T = nc.tensor
    V = nc.vector
    S = nc.scalar
    SY = nc.sync
    G = nc.gpsimd

    def barrier():
        for e in (T, V, S, SY, G):
            for s in SEMS:
                e.wait_ge(s.h, s.v)

    sid = sb("sid", [128, 128], F32)
    sones = sb("sones", [128, 128], F32)
    sH64 = sb("sH64", [128, 64], BF16)
    sH128 = sb("sH128", [128, 128], BF16)
    sA = sb("sA", [128, 32 * NS], F32)
    sdtb = sb("sdtb", [128, 32], F32)
    scvb = sb("scvb", [128, 32], F32)
    scvw = sb("scvw", [128, 32 * 4], F32)
    sD = sb("sD", [128, 32], F32)
    sbsm = sb("sbsm", [64, 8], F32)
    ln_tmp = sb("ln_tmp", [128, D], F32)
    scst = sb("scst", [128, 4], F32)
    for _i, _cv in enumerate((1e-5, 1.0, 1e-8)):
        nc.const_aps.register(F32, float(_cv), scst[:, _i:_i + 1])

    with nc.Block():
        for dst, src in ((sid, ident), (sones, onesc), (sH64, h64c),
                         (sH128, h128c)):
            sW.inc(SY.dma_start(out=dst[:], in_=src), 16)
        sW.inc(SY.dma_start(out=sA.ap().rearrange("p (t n) -> p t n", n=NS),
                            in_=A_t.rearrange("(t p) n -> p t n", p=128)), 16)
        with nc.allow_non_contiguous_dma(reason="tiny one-time vector loads"):
            sW.inc(SY.dma_start(out=sdtb[:],
                                in_=dt_bias.rearrange("(t p) -> p t", p=128)), 16)
            sW.inc(SY.dma_start(out=scvb[:],
                                in_=conv_b.rearrange("(t p) -> p t", p=128)), 16)
        sW.inc(SY.dma_start(out=scvw.ap().rearrange("p (t j) -> p t j", j=4),
                            in_=conv_w.rearrange("(t p) j -> p t j", p=128)), 16)
        with nc.allow_non_contiguous_dma(reason="tiny one-time vector load"):
            sW.inc(SY.dma_start(out=sD[:],
                                in_=D_t.rearrange("(t p) -> p t", p=128)), 16)
        sW.inc(SY.dma_start(out=sbsm[:, 0:1], in_=lam[:, None]), 16)
        sW.inc(SY.dma_start(out=sbsm[:, 1:2], in_=Wa_b[:, None]), 16)
        sW.inc(SY.dma_start(out=sbsm[:, 2:3], in_=norm_w[:, None]), 16)
        for _i, _cv in enumerate((1e-5, 1.0, 1e-8)):
            V.memset(scst[:, _i:_i + 1], float(_cv))
        sV.inc(V.memset(scst[:, 3:4], 0.0))
        barrier()

        with contextlib.ExitStack() as es_p:
            nat = es_p.enter_context(nc.sbuf_tensor("pw_nat", [128, HF], F32))
            tmp = es_p.enter_context(nc.sbuf_tensor("pw_tmp", [128, HF], F32))
            thi = es_p.enter_context(nc.sbuf_tensor("pw_hi", [128, 128], BF16))
            tlo = es_p.enter_context(nc.sbuf_tensor("pw_lo", [128, 128], BF16))
            pps = es_p.enter_context(nc.psum_tensor("pw_ps", [128, 128], F32))

            def split_store(psrc, osz, dma_pairs, scale=1.0, psz=None):
                psz = 128 if psz is None else psz
                if scale != 1.0:
                    S.activation(out=thi[0:psz, 0:osz], in_=psrc, func=AF.Copy,
                                 scale=float(scale))
                    sAC.inc(S.activation(out=ln_tmp[0:psz, 0:osz], in_=psrc,
                                         func=AF.Copy, scale=float(scale)))
                    V.wait_ge(sAC.h, sAC.v)
                    sV.inc(V.tensor_tensor(out=tlo[0:psz, 0:osz],
                                           in0=ln_tmp[0:psz, 0:osz],
                                           in1=thi[0:psz, 0:osz],
                                           op=OP.subtract))
                else:
                    sAC.inc(S.activation(out=thi[0:psz, 0:osz], in_=psrc,
                                         func=AF.Copy))
                    V.wait_ge(sAC.h, sAC.v)
                    sV.inc(V.tensor_tensor(out=tlo[0:psz, 0:osz], in0=psrc,
                                           in1=thi[0:psz, 0:osz],
                                           op=OP.subtract))
                G.wait_ge(sV.h, sV.v)
                for dsta, srca in dma_pairs:
                    sG.inc(G.dma_start(out=dsta, in_=srca), 16)

            def prep_weight(Wsrc, O, I, dst, fold_tile=0, scale=1.0,
                            dst_block=False):
                for ot in range(O // 128):
                    SY.wait_ge(sG.h, sG.v)
                    SY.wait_ge(sPE.h, sPE.v)
                    sW.inc(SY.dma_start(out=nat[:, 0:I],
                                        in_=Wsrc[ot * 128:(ot + 1) * 128, :]), 16)
                    V.wait_ge(sW.h, sW.v)
                    buf = nat
                    if fold_tile:
                        ft = fold_tile
                        nt = I // ft
                        other = tmp
                        for st in range(nt.bit_length() - 1):
                            step = 1 << st
                            npair = nt // (2 * step)
                            a0 = buf.rearrange("p (q two r) -> p q two r",
                                               two=2, r=step * ft)
                            o0 = other.rearrange("p (q two r) -> p q two r",
                                                 two=2, r=step * ft)
                            V.tensor_tensor(out=o0[:, 0:npair, 0, :],
                                            in0=a0[:, 0:npair, 0, :],
                                            in1=a0[:, 0:npair, 1, :], op=OP.add)
                            V.tensor_tensor(out=o0[:, 0:npair, 1, :],
                                            in0=a0[:, 0:npair, 0, :],
                                            in1=a0[:, 0:npair, 1, :],
                                            op=OP.subtract)
                            buf, other = other, buf
                    sV.inc(V.tensor_copy(ln_tmp[:, 0:1], buf[:, 0:1]))
                    T.wait_ge(sV.h, sV.v)
                    for it in range(I // 128):
                        T.wait_ge(sG.h, sG.v)
                        T.wait_ge(sAC.h, sAC.v)
                        sPE.inc(T.transpose(pps[:],
                                            buf[:, it * 128:(it + 1) * 128],
                                            sid[:]))
                        S.wait_ge(sPE.h, sPE.v)
                        V.wait_ge(sPE.h, sPE.v)
                        if dst_block:
                            pairs = [(dst[0][ot, :, it, :], thi[:]),
                                     (dst[1][ot, :, it, :], tlo[:])]
                        else:
                            pairs = [(dst[0][it * 128:(it + 1) * 128,
                                             ot * 128:(ot + 1) * 128], thi[:]),
                                     (dst[1][it * 128:(it + 1) * 128,
                                             ot * 128:(ot + 1) * 128], tlo[:])]
                        split_store(pps[:], 128, pairs, scale)
                barrier()

            prep_weight(W_in, 2 * E, D, WinT, dst_block=True)
            prep_weight(W1, HF, D, W1T, dst_block=True)
            prep_weight(W3, HF, D, W3T, dst_block=True)
            prep_weight(W_dt, E, DTR, WdtT)
            prep_weight(W_out, D, E, WofT, fold_tile=64, scale=RSQN)
            prep_weight(W2, D, HF, W2fT, fold_tile=128, scale=RSQH)
            for o0, osz in ((0, 128), (128, 32)):
                SY.wait_ge(sPE.h, sPE.v)
                SY.wait_ge(sG.h, sG.v)
                sW.inc(SY.dma_start(out=nat[0:osz, 0:E],
                                    in_=W_x[o0:o0 + osz, :]), 16)
                T.wait_ge(sW.h, sW.v)
                for it in range(E // 128):
                    T.wait_ge(sG.h, sG.v)
                    T.wait_ge(sAC.h, sAC.v)
                    sPE.inc(T.transpose(pps[:, 0:osz],
                                        nat[0:osz, it * 128:(it + 1) * 128],
                                        sid[0:osz, 0:osz]))
                    S.wait_ge(sPE.h, sPE.v)
                    V.wait_ge(sPE.h, sPE.v)
                    split_store(pps[:, 0:osz], osz,
                                [(WxT[0][it * 128:(it + 1) * 128, o0:o0 + osz],
                                  thi[:, 0:osz]),
                                 (WxT[1][it * 128:(it + 1) * 128, o0:o0 + osz],
                                  tlo[:, 0:osz])])
                barrier()
            for Wsrc, dsts in ((Wk, WbT["WkT"]), (Wv, WbT["WvT"]),
                               (Wq, WbT["WqT"]), (Wa_w, WbT["WaT"])):
                SY.wait_ge(sPE.h, sPE.v)
                SY.wait_ge(sG.h, sG.v)
                sW.inc(SY.dma_start(out=nat[0:DC, 0:D], in_=Wsrc), 16)
                T.wait_ge(sW.h, sW.v)
                for it in range(16):
                    T.wait_ge(sG.h, sG.v)
                    sPE.inc(T.transpose(pps[:, 0:DC],
                                        nat[0:DC, it * 128:(it + 1) * 128],
                                        sid[0:DC, 0:DC]))
                    S.wait_ge(sPE.h, sPE.v)
                    V.wait_ge(sPE.h, sPE.v)
                    split_store(pps[:, 0:DC], DC,
                                [(dsts[0][it * 128:(it + 1) * 128, :],
                                  thi[:, 0:DC]),
                                 (dsts[1][it * 128:(it + 1) * 128, :],
                                  tlo[:, 0:DC])])
                barrier()
            for it in range(16):
                SY.wait_ge(sPE.h, sPE.v)
                SY.wait_ge(sG.h, sG.v)
                sW.inc(SY.dma_start(out=nat[:, 0:DC],
                                    in_=Wo[it * 128:(it + 1) * 128, :]), 16)
                T.wait_ge(sW.h, sW.v)
                T.wait_ge(sG.h, sG.v)
                sPE.inc(T.transpose(pps[0:DC, :], nat[:, 0:DC], sid[:]))
                S.wait_ge(sPE.h, sPE.v)
                V.wait_ge(sPE.h, sPE.v)
                split_store(pps[0:DC, :], 128,
                            [(WoTs[0][:, it * 128:(it + 1) * 128],
                              thi[0:DC, 0:128]),
                             (WoTs[1][:, it * 128:(it + 1) * 128],
                              tlo[0:DC, 0:128])], psz=DC)
            barrier()
        barrier()

        def VS(inst):
            sV.inc(inst)
            V.wait_ge(sV.h, sV.v)
            return inst

        def SS(inst):
            sAC.inc(inst)
            S.wait_ge(sAC.h, sAC.v)
            return inst

        def ln_fm(src_tok, lnh, lnl, stat, pps2, nlt, ncols, dump=None):
            for j in range(nlt):
                xs = src_tok[:, j * D:(j + 1) * D]
                VS(V.reduce_sum(out=stat[:, 0:1], in_=xs, axis=AX.X))
                sAC.inc(S.activation(out=ln_tmp[:, 0:D], in_=xs, func=AF.Square,
                                     accum_out=stat[:, 1:2]))
                V.wait_ge(sAC.h, sAC.v)
                VS(V.tensor_scalar(out=stat[:, 0:1], in0=stat[:, 0:1],
                                   scalar1=1.0 / D, scalar2=None, op0=OP.mult))
                VS(V.tensor_tensor(out=stat[:, 2:3], in0=stat[:, 0:1],
                                   in1=stat[:, 0:1], op=OP.mult))
                sV.inc(V.scalar_tensor_tensor(out=stat[:, 3:4], in0=stat[:, 1:2],
                                              scalar=1.0 / D, in1=stat[:, 2:3],
                                              op0=OP.mult, op1=OP.subtract))
                S.wait_ge(sV.h, sV.v)
                sAC.inc(S.activation(out=stat[:, 3:4], in_=stat[:, 3:4],
                                     func=AF.Sqrt, bias=1e-5))
                V.wait_ge(sAC.h, sAC.v)
                VS(V.reciprocal(out=stat[:, 3:4], in_=stat[:, 3:4]))
                sV.inc(V.tensor_scalar(out=ln_tmp[:, 0:D], in0=xs,
                                       scalar1=stat[:, 0:1], scalar2=stat[:, 3:4],
                                       op0=OP.subtract, op1=OP.mult))
                T.wait_ge(sV.h, sV.v)
                if dump is not None:
                    G.wait_ge(sV.h, sV.v)
                    sG.inc(G.dma_start(out=dump[j * 128:(j + 1) * 128, :],
                                       in_=ln_tmp[:, 0:D]), 16)
                    sG.inc(G.dma_start(out=d_st[j * 128:(j + 1) * 128, :],
                                       in_=stat[:, 0:8]), 16)
                    for _e in (T, V, S, SY, G):
                        _e.wait_ge(sG.h, sG.v)
                for t in range(D // 128):
                    T.wait_ge(sV.h, sV.v)
                    sPE.inc(T.transpose(pps2[:], ln_tmp[:, t * 128:(t + 1) * 128],
                                        sid[:]))
                    S.wait_ge(sPE.h, sPE.v)
                    co = t * ncols + j * 128
                    sAC.inc(S.activation(out=lnh[:, co:co + 128], in_=pps2[:],
                                         func=AF.Copy))
                    V.wait_ge(sAC.h, sAC.v)
                    sV.inc(V.tensor_tensor(out=lnl[:, co:co + 128], in0=pps2[:],
                                           in1=lnh[:, co:co + 128],
                                           op=OP.subtract))

        def mm3(pst, lh, ll, rh, rl, first, last):
            T.matmul(pst, lh, rh, start=first, stop=False)
            T.matmul(pst, ll, rh, start=False, stop=False)
            return T.matmul(pst, lh, rl, start=False, stop=last)

        def quant_ints(psrc, idst, S_, V_, bscr):
            n = idst.shape[1]
            w = szt_q[:, 0:n]
            b = bscr[:, 0:n]
            sAC.inc(S_.activation(out=idst, in_=psrc, func=AF.Copy,
                                  scale=20.0, bias=130.5))
            V_.wait_ge(sAC.h, sAC.v)
            V_.tensor_scalar(out=idst, in0=idst, scalar1=1.0, scalar2=259.0,
                             op0=OP.max, op1=OP.min)
            V_.tensor_copy(w, idst)
            for c in (256.0, 128.0, 64.0, 32.0, 16.0, 8.0, 4.0, 2.0, 1.0):
                V_.tensor_scalar(out=b, in0=w, scalar1=c, scalar2=None,
                                 op0=OP.is_ge)
                V_.scalar_tensor_tensor(out=w, in0=b, scalar=-c, in1=w,
                                        op0=OP.mult, op1=OP.add)
            V_.tensor_tensor(out=idst, in0=idst, in1=w, op=OP.subtract)
            V_.tensor_scalar(out=idst, in0=idst, scalar1=-130.0,
                             scalar2=-128.0, op0=OP.add, op1=OP.max)
            sV.inc(V_.tensor_scalar(out=idst, in0=idst, scalar1=127.0,
                                    scalar2=None, op0=OP.min))

        NLT = LCA // 128
        szt_q = sb("szt_q", [128, LCA], F32)
        with contextlib.ExitStack() as es_a:
            xt = es_a.enter_context(nc.sbuf_tensor("a_xt", [128, NLT * D], F32))
            xq = es_a.enter_context(
                nc.sbuf_tensor("a_xq", [128, 32 * (LCA + 4)], F32))
            stat = es_a.enter_context(nc.sbuf_tensor("a_stat", [128, 8], F32))
            lnh = es_a.enter_context(nc.sbuf_tensor("a_lnh", [128, 16 * LCA], BF16))
            lnl = es_a.enter_context(nc.sbuf_tensor("a_lnl", [128, 16 * LCA], BF16))
            u = es_a.enter_context(nc.sbuf_tensor("a_u", [128, 32 * LCA], F32))
            ui = es_a.enter_context(nc.sbuf_tensor("a_ui", [128, 32 * LCA], BF16))
            dtf = es_a.enter_context(nc.sbuf_tensor("a_dt", [128, 32 * LCA], F32))
            dtih = es_a.enter_context(nc.sbuf_tensor("a_dtih", [128, LCA], BF16))
            dtil = es_a.enter_context(nc.sbuf_tensor("a_dtil", [128, LCA], BF16))
            bc = es_a.enter_context(nc.sbuf_tensor("a_bc", [32, LCA], F32))
            browA = es_a.enter_context(
                nc.sbuf_tensor("a_brow", [1, 2 * LCA], F32))
            state = es_a.enter_context(nc.sbuf_tensor("a_state", [128, 32 * NS], F32))
            dA = es_a.enter_context(nc.sbuf_tensor("a_dA", [128, LCA], F32))
            pbb = es_a.enter_context(nc.sbuf_tensor("a_pb", [128, LCA], F32))
            hbf = es_a.enter_context(nc.sbuf_tensor("a_h", [128, LCA], F32))
            hcc = es_a.enter_context(nc.sbuf_tensor("a_hc", [128, LCA], F32))
            szt = es_a.enter_context(nc.sbuf_tensor("a_sz", [128, LCA], F32))
            z1h = es_a.enter_context(nc.sbuf_tensor("a_z1h", [128, LCA], BF16))
            z1l = es_a.enter_context(nc.sbuf_tensor("a_z1l", [128, LCA], BF16))
            wsh = es_a.enter_context(nc.sbuf_tensor("a_wsh", [128, 5120], BF16))
            wsl = es_a.enter_context(nc.sbuf_tensor("a_wsl", [128, 5120], BF16))
            wmh = es_a.enter_context(nc.sbuf_tensor("a_wmh", [128, 512], BF16))
            wml = es_a.enter_context(nc.sbuf_tensor("a_wml", [128, 512], BF16))
            p0 = es_a.enter_context(nc.psum_tensor("a_p0", [128, LCA], F32))
            p1 = es_a.enter_context(nc.psum_tensor("a_p1", [128, LCA], F32))
            pt2 = es_a.enter_context(nc.psum_tensor("a_pt", [128, 128], F32))
            po0 = es_a.enter_context(nc.psum_tensor("a_po0", [128, 512], F32))
            po1 = es_a.enter_context(nc.psum_tensor("a_po1", [128, 512], F32))

            V.memset(state[:], 0.0)
            sV.inc(V.memset(xq[:], 0.0))
            barrier()

            def xqs(t):
                return xq[:, t * (LCA + 4) + 4:(t + 1) * (LCA + 4)]

            yvs = xqs

            for c in range(L // LCA):
                SY.wait_ge(sV.h, sV.v)
                SY.wait_ge(sG.h, sG.v)
                for j in range(NLT):
                    sW.inc(SY.dma_start(
                        out=xt[:, j * D:(j + 1) * D],
                        in_=x_in[c * LCA + j * 128:c * LCA + (j + 1) * 128, :]),
                        16)
                V.wait_ge(sW.h, sW.v)
                S.wait_ge(sW.h, sW.v)
                ln_fm(xt, lnh, lnl, stat, pt2, NLT, LCA,
                      dump=(d_lt if (dbg and c == 0) else None))
                barrier()
                if dbg and c == 0:
                    for t in range(16):
                        V.tensor_tensor(out=szt[:], in0=lnh[:, t * LCA:(t + 1) * LCA],
                                        in1=lnl[:, t * LCA:(t + 1) * LCA], op=OP.add)
                        sV.inc(V.tensor_copy(hcc[:], szt[:]))
                        G.wait_ge(sV.h, sV.v)
                        sG.inc(G.dma_start(
                            out=d_ln[t * 128:(t + 1) * 128, :],
                            in_=hcc[:]), 16)
                    barrier()
                for m in range(32):
                    SY.wait_ge(sPE.h, sPE.v)
                    sW.inc(SY.dma_start(out=wsh[:, 0:2048], in_=WinT[0][m]), 16)
                    sW.inc(SY.dma_start(out=wsl[:, 0:2048], in_=WinT[1][m]), 16)
                    T.wait_ge(sW.h, sW.v)
                    T.wait_ge(sAC.h, sAC.v)
                    for k in range(16):
                        mi = mm3(p0[:], wsh[:, k * 128:(k + 1) * 128],
                                 wsl[:, k * 128:(k + 1) * 128],
                                 lnh[:, k * LCA:(k + 1) * LCA],
                                 lnl[:, k * LCA:(k + 1) * LCA], k == 0, k == 15)
                    sPE.inc(mi)
                    S.wait_ge(sPE.h, sPE.v)
                    quant_ints(p0[:], xqs(m), S, V, hbf)
                for t in range(32):
                    xx = xq[:, t * (LCA + 4):(t + 1) * (LCA + 4)]
                    V.wait_ge(sAC.h, sAC.v)
                    V.tensor_scalar(out=dA[:], in0=xx[:, 1:1 + LCA],
                                    scalar1=scvw[:, 4 * t:4 * t + 1],
                                    scalar2=None, op0=OP.mult)
                    for j in (1, 2, 3):
                        V.scalar_tensor_tensor(
                            out=dA[:], in0=xx[:, 1 + j:1 + j + LCA],
                            scalar=scvw[:, 4 * t + j:4 * t + j + 1],
                            in1=dA[:], op0=OP.mult, op1=OP.add)
                    sV.inc(V.tensor_copy(xx[:, 1:4], xx[:, LCA + 1:LCA + 4]))
                    S.wait_ge(sV.h, sV.v)
                    sAC.inc(S.activation(out=pbb[:], in_=dA[:], func=AF.Silu,
                                         scale=0.05, bias=scvb[:, t:t + 1]))
                    S.wait_ge(sV.h, sV.v)
                    quant_ints(pbb[:], hcc[:], S, V, hbf)
                    V.tensor_scalar(out=u[:, t * LCA:(t + 1) * LCA], in0=hcc[:],
                                    scalar1=0.05, scalar2=None, op0=OP.mult)
                    sV.inc(V.tensor_copy(ui[:, t * LCA:(t + 1) * LCA], hcc[:]))
                if dbg and c == 0:
                    for t in range(32):
                        sV.inc(V.tensor_copy(hcc[:], xqs(t)))
                        G.wait_ge(sV.h, sV.v)
                        sG.inc(G.dma_start(
                            out=d_xq[t * 128:(t + 1) * 128, :],
                            in_=hcc[:]), 16)
                        sV.inc(V.tensor_copy(hcc[:], u[:, t * LCA:(t + 1) * LCA]))
                        G.wait_ge(sV.h, sV.v)
                        sG.inc(G.dma_start(
                            out=d_u[t * 128:(t + 1) * 128, :],
                            in_=hcc[:]), 16)
                    barrier()
                SY.wait_ge(sPE.h, sPE.v)
                sW.inc(SY.dma_start(
                    out=wsh.ap()[:, 0:32 * 160].rearrange(
                        "p (k o) -> p k o", o=160),
                    in_=WxT[0].rearrange("(k p) o -> p k o", p=128)), 16)
                sW.inc(SY.dma_start(
                    out=wsl.ap()[:, 0:32 * 160].rearrange(
                        "p (k o) -> p k o", o=160),
                    in_=WxT[1].rearrange("(k p) o -> p k o", p=128)), 16)
                T.wait_ge(sW.h, sW.v)
                T.wait_ge(sV.h, sV.v)
                T.wait_ge(sAC.h, sAC.v)
                for m0, msz, pdst in ((0, 128, p0), (128, 32, p1)):
                    for k in range(32):
                        lh = wsh[:, k * 160 + m0:k * 160 + m0 + msz]
                        ll = wsl[:, k * 160 + m0:k * 160 + m0 + msz]
                        rh = ui[:, k * LCA:(k + 1) * LCA]
                        T.matmul(pdst[0:msz, :], lh, rh, start=(k == 0),
                                 stop=False)
                        mi = T.matmul(pdst[0:msz, :], ll, rh, start=False,
                                      stop=(k == 31))
                sPE.inc(mi)
                S.wait_ge(sPE.h, sPE.v)
                sAC.inc(S.activation(out=dtih[:], in_=p0[:], func=AF.Copy,
                                     scale=0.05))
                sAC.inc(S.activation(out=bc[:], in_=p1[0:32, :], func=AF.Copy,
                                     scale=0.05))
                V.wait_ge(sAC.h, sAC.v)
                sV.inc(V.scalar_tensor_tensor(out=dtil[:], in0=p0[:],
                                              scalar=0.05, in1=dtih[:],
                                              op0=OP.mult, op1=OP.subtract))
                SY.wait_ge(sPE.h, sPE.v)
                sW.inc(SY.dma_start(out=wsh[:, 0:E], in_=WdtT[0]), 16)
                sW.inc(SY.dma_start(out=wsl[:, 0:E], in_=WdtT[1]), 16)
                T.wait_ge(sW.h, sW.v)
                T.wait_ge(sV.h, sV.v)
                for m in range(32):
                    sPE.inc(mm3(p0[:], wsh[:, m * 128:(m + 1) * 128],
                                wsl[:, m * 128:(m + 1) * 128], dtih[:], dtil[:],
                                True, True))
                    S.wait_ge(sPE.h, sPE.v)
                    S.activation(out=szt[:], in_=p0[:], func=AF.Exp,
                                 bias=sdtb[:, m:m + 1])
                    sAC.inc(S.activation(out=dtf[:, m * LCA:(m + 1) * LCA],
                                         in_=szt[:], func=AF.Ln, bias=1.0))
                    T.wait_ge(sAC.h, sAC.v)
                if dbg and c == 0:
                    for t in range(32):
                        sV.inc(V.tensor_copy(hcc[:], dtf[:, t * LCA:(t + 1) * LCA]))
                        G.wait_ge(sV.h, sV.v)
                        sG.inc(G.dma_start(
                            out=d_dt[t * 128:(t + 1) * 128, :],
                            in_=hcc[:]), 16)
                    barrier()
                V.wait_ge(sAC.h, sAC.v)
                for t in range(32):
                    V.tensor_scalar(out=yvs(t), in0=u[:, t * LCA:(t + 1) * LCA],
                                    scalar1=sD[:, t:t + 1], scalar2=None,
                                    op0=OP.mult)
                for t in range(32):
                    V.tensor_tensor(out=u[:, t * LCA:(t + 1) * LCA],
                                    in0=dtf[:, t * LCA:(t + 1) * LCA],
                                    in1=u[:, t * LCA:(t + 1) * LCA], op=OP.mult)
                sV.inc(V.tensor_copy(hcc[:, 0:1], u[:, 0:1]))
                for n in range(NS):
                    T.wait_ge(sV.h, sV.v)
                    sPE.inc(T.matmul(p1[0:1, :], sid[0:32, n:n + 1], bc[:],
                                     start=True, stop=True))
                    V.wait_ge(sPE.h, sPE.v)
                    sV.inc(V.tensor_copy(browA[0:1, 0:LCA], p1[0:1, :]))
                    T.wait_ge(sV.h, sV.v)
                    T.matmul(po0[:, 0:LCA], sones[0:1, 0:128],
                             browA[0:1, 0:LCA], start=True, stop=True)
                    sPE.inc(T.matmul(p1[0:1, :], sid[0:32, 16 + n:17 + n],
                                     bc[:], start=True, stop=True))
                    V.wait_ge(sPE.h, sPE.v)
                    sV.inc(V.tensor_copy(browA[0:1, LCA:2 * LCA], p1[0:1, :]))
                    T.wait_ge(sV.h, sV.v)
                    sPE.inc(T.matmul(po1[:, 0:LCA], sones[0:1, 0:128],
                                     browA[0:1, LCA:2 * LCA], start=True,
                                     stop=True))
                    S.wait_ge(sPE.h, sPE.v)
                    V.wait_ge(sPE.h, sPE.v)
                    for t in range(32):
                        S.wait_ge(sV.h, sV.v)
                        sAC.inc(S.activation(
                            out=dA[:], in_=dtf[:, t * LCA:(t + 1) * LCA],
                            func=AF.Exp,
                            scale=sA[:, t * NS + n:t * NS + n + 1]))
                        V.tensor_tensor(out=pbb[:],
                                        in0=u[:, t * LCA:(t + 1) * LCA],
                                        in1=po0[:, 0:LCA], op=OP.mult)
                        V.wait_ge(sAC.h, sAC.v)
                        VS(V.tensor_tensor_scan(
                            out=hbf[:], data0=dA[:], data1=pbb[:],
                            initial=state[:, t * NS + n:t * NS + n + 1],
                            op0=OP.mult, op1=OP.add))
                        V.tensor_copy(state[:, t * NS + n:t * NS + n + 1],
                                      hbf[:, LCA - 1:LCA])
                        V.tensor_tensor(out=hcc[:], in0=hbf[:], in1=po1[:, 0:LCA],
                                        op=OP.mult)
                        sV.inc(V.tensor_tensor(out=yvs(t), in0=yvs(t),
                                               in1=hcc[:], op=OP.add))
                for m in range(32):
                    SY.wait_ge(sPE.h, sPE.v)
                    sW.inc(SY.dma_start(out=wsh[:, 0:2048], in_=WinT[0][32 + m]), 16)
                    sW.inc(SY.dma_start(out=wsl[:, 0:2048], in_=WinT[1][32 + m]), 16)
                    T.wait_ge(sW.h, sW.v)
                    T.wait_ge(sAC.h, sAC.v)
                    for k in range(16):
                        mi = mm3(p0[:], wsh[:, k * 128:(k + 1) * 128],
                                 wsl[:, k * 128:(k + 1) * 128],
                                 lnh[:, k * LCA:(k + 1) * LCA],
                                 lnl[:, k * LCA:(k + 1) * LCA], k == 0, k == 15)
                    sPE.inc(mi)
                    S.wait_ge(sPE.h, sPE.v)
                    sAC.inc(S.activation(out=szt[:], in_=p0[:], func=AF.Silu))
                    V.wait_ge(sAC.h, sAC.v)
                    sV.inc(V.tensor_tensor(out=yvs(m), in0=yvs(m), in1=szt[:],
                                           op=OP.mult))
                if dbg and c == 0:
                    for t in range(32):
                        sV.inc(V.tensor_copy(hcc[:], yvs(t)))
                        G.wait_ge(sV.h, sV.v)
                        sG.inc(G.dma_start(
                            out=d_y[t * 128:(t + 1) * 128, :],
                            in_=hcc[:]), 16)
                    barrier()
                for t in range(32):
                    V.tensor_copy(ui[:, t * LCA:(t + 1) * LCA], yvs(t))
                for t in range(32):
                    dst = lnh if t < 16 else lnl
                    V.tensor_tensor(out=dst[:, (t % 16) * LCA:(t % 16 + 1) * LCA],
                                    in0=yvs(t),
                                    in1=ui[:, t * LCA:(t + 1) * LCA],
                                    op=OP.subtract)
                sV.inc(V.tensor_copy(hcc[:, 0:1], ui[:, 0:1]))

                def yh_t(t):
                    return ui[:, t * LCA:(t + 1) * LCA]

                def yl_t(t):
                    dst = lnh if t < 16 else lnl
                    return dst[:, (t % 16) * LCA:(t % 16 + 1) * LCA]

                T.wait_ge(sV.h, sV.v)
                for dn in range(4):
                    for k in range(32):
                        T.wait_ge(sAC.h, sAC.v)
                        T.wait_ge(sV.h, sV.v)
                        T.matmul(p0[0:64, :], sH64[0:64, :], yh_t(k)[0:64, :],
                                 start=True, stop=False)
                        T.matmul(p0[0:64, :], sH64[0:64, :], yl_t(k)[0:64, :],
                                 start=False, stop=True)
                        T.matmul(p0[64:128, :], sH64[64:128, :],
                                 yh_t(k)[64:128, :],
                                 start=True, stop=False, tile_position=(64, 64))
                        sPE.inc(T.matmul(p0[64:128, :], sH64[64:128, :],
                                         yl_t(k)[64:128, :], start=False,
                                         stop=True, tile_position=(64, 64)))
                        S.wait_ge(sPE.h, sPE.v)
                        sAC.inc(S.activation(out=z1h[:], in_=p0[:], func=AF.Copy))
                        V.wait_ge(sAC.h, sAC.v)
                        sV.inc(V.tensor_tensor(out=z1l[:], in0=p0[:], in1=z1h[:],
                                               op=OP.subtract))
                        SY.wait_ge(sPE.h, sPE.v)
                        sW.inc(SY.dma_start(
                            out=wmh[:], in_=WofT[0][k * 128:(k + 1) * 128,
                                                    dn * 512:(dn + 1) * 512]), 16)
                        sW.inc(SY.dma_start(
                            out=wml[:], in_=WofT[1][k * 128:(k + 1) * 128,
                                                    dn * 512:(dn + 1) * 512]), 16)
                        T.wait_ge(sW.h, sW.v)
                        T.wait_ge(sV.h, sV.v)
                        for lm in range(NLT):
                            pd = po0 if lm == 0 else po1
                            zh = z1h[:, lm * 128:(lm + 1) * 128]
                            zl = z1l[:, lm * 128:(lm + 1) * 128]
                            T.matmul(pd[:], zh, wmh[:], start=(k == 0),
                                     stop=False)
                            T.matmul(pd[:], zl, wmh[:], start=False, stop=False)
                            mi = T.matmul(pd[:], zh, wml[:], start=False,
                                          stop=(k == 31))
                        sPE.inc(mi)
                    V.wait_ge(sPE.h, sPE.v)
                    for lm in range(NLT):
                        pd = po0 if lm == 0 else po1
                        sl = slice(lm * D + dn * 512, lm * D + (dn + 1) * 512)
                        sV.inc(V.tensor_tensor(out=xt[:, sl], in0=xt[:, sl],
                                               in1=pd[:], op=OP.add))
                    T.wait_ge(sV.h, sV.v)
                G.wait_ge(sV.h, sV.v)
                for j in range(NLT):
                    sG.inc(G.dma_start(
                        out=X1s[c * LCA + j * 128:c * LCA + (j + 1) * 128, :],
                        in_=xt[:, j * D:(j + 1) * D]), 16)
                barrier()
            barrier()
        barrier()

        def scratch_to_out(src):
            with nc.sbuf_tensor("cp_t", [128, D], F32) as cpt:
                for c in range(L // 128):
                    SY.wait_ge(sG.h, sG.v)
                    sW.inc(SY.dma_start(out=cpt[:],
                                        in_=src[c * 128:(c + 1) * 128, :]), 16)
                    G.wait_ge(sW.h, sW.v)
                    sG.inc(G.dma_start(out=out[c * 128:(c + 1) * 128, :],
                                       in_=cpt[:]), 16)
                G.wait_ge(sG.h, sG.v)
                barrier()

        if stop_stage == 1:
            scratch_to_out(X1s)
            ctx.close()
            return nc

        _ffn_n = [0]

        def ffn_stage(Xin_s, Xout_s, to_out):
            NLF = LCF // 128
            fx = f"g{_ffn_n[0]}_"
            _ffn_n[0] += 1
            with contextlib.ExitStack() as es_f:
                fxt = es_f.enter_context(nc.sbuf_tensor(fx + "f_xt", [128, NLF * D], F32))
                fstat = es_f.enter_context(nc.sbuf_tensor(fx + "f_stat", [128, 8], F32))
                flnh = es_f.enter_context(nc.sbuf_tensor(fx + "f_lnh", [128, 16 * LCF], BF16))
                flnl = es_f.enter_context(nc.sbuf_tensor(fx + "f_lnl", [128, 16 * LCF], BF16))
                fz1h = es_f.enter_context(nc.sbuf_tensor(fx + "f_z1h", [128, 64 * LCF], BF16))
                fz1l = es_f.enter_context(nc.sbuf_tensor(fx + "f_z1l", [128, 64 * LCF], BF16))
                fhh = es_f.enter_context(nc.sbuf_tensor(fx + "f_hh", [128, LCF], BF16))
                fhl = es_f.enter_context(nc.sbuf_tensor(fx + "f_hl", [128, LCF], BF16))
                ft0 = es_f.enter_context(nc.sbuf_tensor(fx + "f_t0", [128, LCF], F32))
                ft1 = es_f.enter_context(nc.sbuf_tensor(fx + "f_t1", [128, LCF], F32))
                fw1h = es_f.enter_context(nc.sbuf_tensor(fx + "f_w1h", [128, 16 * 128], BF16))
                fw1l = es_f.enter_context(nc.sbuf_tensor(fx + "f_w1l", [128, 16 * 128], BF16))
                fw3h = es_f.enter_context(nc.sbuf_tensor(fx + "f_w3h", [128, 16 * 128], BF16))
                fw3l = es_f.enter_context(nc.sbuf_tensor(fx + "f_w3l", [128, 16 * 128], BF16))
                fwmh = es_f.enter_context(nc.sbuf_tensor(fx + "f_wmh", [128, 512], BF16))
                fwml = es_f.enter_context(nc.sbuf_tensor(fx + "f_wml", [128, 512], BF16))
                fp0 = es_f.enter_context(nc.psum_tensor(fx + "f_p0", [128, LCF], F32))
                fp1 = es_f.enter_context(nc.psum_tensor(fx + "f_p1", [128, LCF], F32))
                fpt = es_f.enter_context(nc.psum_tensor(fx + "f_pt", [128, 128], F32))
                fpo0 = es_f.enter_context(nc.psum_tensor(fx + "f_po0", [128, 512], F32))
                fpo1 = es_f.enter_context(nc.psum_tensor(fx + "f_po1", [128, 512], F32))
                for c in range(L // LCF):
                    SY.wait_ge(sG.h, sG.v)
                    for j in range(NLF):
                        sW.inc(SY.dma_start(
                            out=fxt[:, j * D:(j + 1) * D],
                            in_=Xin_s[c * LCF + j * 128:
                                      c * LCF + (j + 1) * 128, :]), 16)
                    V.wait_ge(sW.h, sW.v)
                    S.wait_ge(sW.h, sW.v)
                    ln_fm(fxt, flnh, flnl, fstat, fpt, NLF, LCF)
                    barrier()
                    for m in range(64):
                        SY.wait_ge(sPE.h, sPE.v)
                        sW.inc(SY.dma_start(out=fw1h[:], in_=W1T[0][m]), 16)
                        sW.inc(SY.dma_start(out=fw1l[:], in_=W1T[1][m]), 16)
                        sW.inc(SY.dma_start(out=fw3h[:], in_=W3T[0][m]), 16)
                        sW.inc(SY.dma_start(out=fw3l[:], in_=W3T[1][m]), 16)
                        T.wait_ge(sW.h, sW.v)
                        T.wait_ge(sV.h, sV.v)
                        for k in range(16):
                            mm3(fp0[:], fw1h[:, k * 128:(k + 1) * 128],
                                fw1l[:, k * 128:(k + 1) * 128],
                                flnh[:, k * LCF:(k + 1) * LCF],
                                flnl[:, k * LCF:(k + 1) * LCF], k == 0, k == 15)
                        for k in range(16):
                            mi = mm3(fp1[:], fw3h[:, k * 128:(k + 1) * 128],
                                     fw3l[:, k * 128:(k + 1) * 128],
                                     flnh[:, k * LCF:(k + 1) * LCF],
                                     flnl[:, k * LCF:(k + 1) * LCF],
                                     k == 0, k == 15)
                        sPE.inc(mi)
                        S.wait_ge(sPE.h, sPE.v)
                        sAC.inc(S.activation(out=ft0[:], in_=fp0[:],
                                             func=AF.Silu))
                        V.wait_ge(sAC.h, sAC.v)
                        V.tensor_tensor(out=ft1[:], in0=ft0[:], in1=fp1[:],
                                        op=OP.mult)
                        V.tensor_copy(fhh[:], ft1[:])
                        sV.inc(V.tensor_tensor(out=fhl[:], in0=ft1[:],
                                               in1=fhh[:], op=OP.subtract))
                        T.wait_ge(sV.h, sV.v)
                        T.matmul(fp0[:], sH128[:], fhh[:], start=True,
                                 stop=False)
                        sPE.inc(T.matmul(fp0[:], sH128[:], fhl[:], start=False,
                                         stop=True))
                        S.wait_ge(sPE.h, sPE.v)
                        sAC.inc(S.activation(out=fz1h[:, m * LCF:(m + 1) * LCF],
                                             in_=fp0[:], func=AF.Copy))
                        V.wait_ge(sAC.h, sAC.v)
                        sV.inc(V.tensor_tensor(
                            out=fz1l[:, m * LCF:(m + 1) * LCF], in0=fp0[:],
                            in1=fz1h[:, m * LCF:(m + 1) * LCF],
                            op=OP.subtract))
                        T.wait_ge(sV.h, sV.v)
                    for dn in range(4):
                        for k in range(64):
                            SY.wait_ge(sPE.h, sPE.v)
                            sW.inc(SY.dma_start(
                                out=fwmh[:],
                                in_=W2fT[0][k * 128:(k + 1) * 128,
                                            dn * 512:(dn + 1) * 512]), 16)
                            sW.inc(SY.dma_start(
                                out=fwml[:],
                                in_=W2fT[1][k * 128:(k + 1) * 128,
                                            dn * 512:(dn + 1) * 512]), 16)
                            T.wait_ge(sW.h, sW.v)
                            for lm in range(NLF):
                                pd = fpo0 if lm == 0 else fpo1
                                zh = fz1h[:, k * LCF + lm * 128:
                                          k * LCF + (lm + 1) * 128]
                                zl = fz1l[:, k * LCF + lm * 128:
                                          k * LCF + (lm + 1) * 128]
                                T.matmul(pd[:], zh, fwmh[:], start=(k == 0),
                                         stop=False)
                                T.matmul(pd[:], zl, fwmh[:], start=False,
                                         stop=False)
                                mi = T.matmul(pd[:], zh, fwml[:], start=False,
                                              stop=(k == 63))
                            sPE.inc(mi)
                        V.wait_ge(sPE.h, sPE.v)
                        for lm in range(NLF):
                            pd = fpo0 if lm == 0 else fpo1
                            sl = slice(lm * D + dn * 512,
                                       lm * D + (dn + 1) * 512)
                            sV.inc(V.tensor_tensor(out=fxt[:, sl],
                                                   in0=fxt[:, sl], in1=pd[:],
                                                   op=OP.add))
                        T.wait_ge(sV.h, sV.v)
                    G.wait_ge(sV.h, sV.v)
                    dst = out if to_out else Xout_s
                    for j in range(NLF):
                        sG.inc(G.dma_start(
                            out=dst[c * LCF + j * 128:c * LCF + (j + 1) * 128,
                                    :],
                            in_=fxt[:, j * D:(j + 1) * D]), 16)
                    barrier()
                barrier()

        ffn_stage(X1s, X2s, to_out=(stop_stage == 2))
        if stop_stage == 2:
            G.wait_ge(sG.h, sG.v)
            ctx.close()
            return nc

        with contextlib.ExitStack() as es_b:
            bxt = es_b.enter_context(nc.sbuf_tensor("b_xt", [128, 2 * D], F32))
            bstat = es_b.enter_context(nc.sbuf_tensor("b_stat", [128, 8], F32))
            blnh = es_b.enter_context(nc.sbuf_tensor("b_lnh", [128, 16 * 256], BF16))
            blnl = es_b.enter_context(nc.sbuf_tensor("b_lnl", [128, 16 * 256], BF16))
            bK = es_b.enter_context(nc.sbuf_tensor("b_K", [64, L], F32))
            bV = es_b.enter_context(nc.sbuf_tensor("b_V", [64, L], F32))
            bQ = es_b.enter_context(nc.sbuf_tensor("b_Q", [64, L], F32))
            bA = es_b.enter_context(nc.sbuf_tensor("b_A", [64, L], F32))
            bGK = es_b.enter_context(nc.sbuf_tensor("b_GK", [64, L], F32))
            bsqf = es_b.enter_context(nc.sbuf_tensor("b_sqf", [64, L], F32))
            brow = es_b.enter_context(nc.sbuf_tensor("b_row", [1, L], F32))
            bVr = es_b.enter_context(
                nc.sbuf_tensor("b_Vr", [128, 64 * DC], F32))
            bGKr = es_b.enter_context(
                nc.sbuf_tensor("b_GKr", [128, 64 * DC], F32))
            bYr = es_b.enter_context(
                nc.sbuf_tensor("b_Yr", [128, 64 * DC], F32))
            bH = es_b.enter_context(nc.sbuf_tensor("b_H", [64, 64], F32))
            bHd = es_b.enter_context(nc.sbuf_tensor("b_Hd", [64, 64], F32))
            bwe = es_b.enter_context(nc.sbuf_tensor("b_we", [1, 128], F32))
            bsq = es_b.enter_context(nc.sbuf_tensor("b_sq", [1, 64], F32))
            berr = es_b.enter_context(nc.sbuf_tensor("b_err", [1, 1], F32))
            bs = es_b.enter_context(nc.sbuf_tensor("b_s", [1, 1], F32))
            bu = es_b.enter_context(nc.sbuf_tensor("b_u", [1, 64], F32))
            bYt = es_b.enter_context(nc.sbuf_tensor("b_yt", [128, 16 * DC], F32))
            bynh = es_b.enter_context(nc.sbuf_tensor("b_ynh", [64, L], BF16))
            bynl = es_b.enter_context(nc.sbuf_tensor("b_ynl", [64, L], BF16))
            bp0 = es_b.enter_context(nc.psum_tensor("b_p0", [128, 512], F32))
            bpk1 = es_b.enter_context(nc.psum_tensor("b_pk1", [1, 64], F32))
            bpk2 = es_b.enter_context(nc.psum_tensor("b_pk2", [1, 64], F32))
            bpO = es_b.enter_context(nc.psum_tensor("b_pO", [64, 64], F32))
            bpY = es_b.enter_context(nc.psum_tensor("b_pY", [1, 64], F32))
            bpt = es_b.enter_context(nc.psum_tensor("b_pt", [128, 128], F32))
            sWkT = [es_b.enter_context(
                nc.sbuf_tensor(f"zWk{hl}", [128, 16 * DC], BF16))
                for hl in "hl"]
            sWvT = [es_b.enter_context(
                nc.sbuf_tensor(f"zWv{hl}", [128, 16 * DC], BF16))
                for hl in "hl"]
            sWqT = [es_b.enter_context(
                nc.sbuf_tensor(f"zWq{hl}", [128, 16 * DC], BF16))
                for hl in "hl"]
            sWaT = [es_b.enter_context(
                nc.sbuf_tensor(f"zWa{hl}", [128, 16 * DC], BF16))
                for hl in "hl"]
            sWoT = [es_b.enter_context(
                nc.sbuf_tensor(f"zWo{hl}", [64, D], BF16)) for hl in "hl"]
            for sdst, ssrc in ((sWkT, WbT["WkT"]), (sWvT, WbT["WvT"]),
                               (sWqT, WbT["WqT"]), (sWaT, WbT["WaT"])):
                for hl in range(2):
                    sW.inc(SY.dma_start(
                        out=sdst[hl].ap().rearrange("p (k o) -> p k o", o=DC),
                        in_=ssrc[hl].rearrange("(k p) o -> p k o", p=128)), 16)
            for hl in range(2):
                sW.inc(SY.dma_start(out=sWoT[hl][:], in_=WoTs[hl]), 16)
            barrier()
            for c in range(L // 256):
                SY.wait_ge(sV.h, sV.v)
                for j in range(2):
                    sW.inc(SY.dma_start(
                        out=bxt[:, j * D:(j + 1) * D],
                        in_=X2s[c * 256 + j * 128:c * 256 + (j + 1) * 128, :]),
                        16)
                V.wait_ge(sW.h, sW.v)
                S.wait_ge(sW.h, sW.v)
                ln_fm(bxt, blnh, blnl, bstat, bpt, 2, 256)
                T.wait_ge(sV.h, sV.v)
                T.wait_ge(sAC.h, sAC.v)
                for dst, wst in ((bK, sWkT), (bV, sWvT), (bQ, sWqT),
                                 (bA, sWaT)):
                    for k in range(16):
                        mi = mm3(bp0[0:64, 0:256],
                                 wst[0][:, k * DC:(k + 1) * DC],
                                 wst[1][:, k * DC:(k + 1) * DC],
                                 blnh[:, k * 256:(k + 1) * 256],
                                 blnl[:, k * 256:(k + 1) * 256], k == 0, k == 15)
                    sPE.inc(mi)
                    V.wait_ge(sPE.h, sPE.v)
                    sV.inc(V.tensor_copy(dst[:, c * 256:(c + 1) * 256],
                                         bp0[0:64, 0:256]))
                    T.wait_ge(sV.h, sV.v)
            barrier()
            NC5 = min(512, L)
            for dst in (bK, bQ):
                sV.inc(V.tensor_tensor(out=bsqf[:], in0=dst[:], in1=dst[:],
                                       op=OP.mult))
                T.wait_ge(sV.h, sV.v)
                for cc in range(L // NC5):
                    sPE.inc(T.matmul(bp0[0:1, 0:NC5], sones[0:64, 0:1],
                                     bsqf[:, cc * NC5:(cc + 1) * NC5],
                                     start=True, stop=True))
                    S.wait_ge(sPE.h, sPE.v)
                    sAC.inc(S.activation(out=brow[0:1, cc * NC5:(cc + 1) * NC5],
                                         in_=bp0[0:1, 0:NC5], func=AF.Sqrt))
                    T.wait_ge(sAC.h, sAC.v)
                V.wait_ge(sAC.h, sAC.v)
                VS(V.tensor_scalar(out=brow[:], in0=brow[:], scalar1=1e-12,
                                   scalar2=None, op0=OP.max))
                sV.inc(V.reciprocal(out=brow[:], in_=brow[:]))
                T.wait_ge(sV.h, sV.v)
                for cc in range(L // NC5):
                    sPE.inc(T.matmul(bp0[0:64, 0:NC5], sones[0:1, 0:64],
                                     brow[0:1, cc * NC5:(cc + 1) * NC5],
                                     start=True, stop=True))
                    V.wait_ge(sPE.h, sPE.v)
                    sV.inc(V.tensor_tensor(out=dst[:, cc * NC5:(cc + 1) * NC5],
                                           in0=dst[:, cc * NC5:(cc + 1) * NC5],
                                           in1=bp0[0:64, 0:NC5], op=OP.mult))
                    T.wait_ge(sV.h, sV.v)
            S.wait_ge(sV.h, sV.v)
            SS(S.activation(out=sbsm[:, 3:4], in_=sbsm[:, 0:1],
                            func=AF.Sigmoid))
            sAC.inc(S.activation(out=sbsm[:, 4:5], in_=sbsm[:, 3:4], func=AF.Ln,
                                 bias=1e-8))
            sAC.inc(S.activation(out=bA[:], in_=bA[:], func=AF.Sigmoid,
                                 bias=sbsm[:, 1:2]))
            V.wait_ge(sAC.h, sAC.v)
            sV.inc(V.tensor_scalar(out=bA[:], in0=bA[:], scalar1=sbsm[:, 4:5],
                                   scalar2=None, op0=OP.mult))
            S.wait_ge(sV.h, sV.v)
            sAC.inc(S.activation(out=bA[:], in_=bA[:], func=AF.Exp, scale=8.0))
            V.wait_ge(sAC.h, sAC.v)
            V.tensor_scalar(out=bGK[:], in0=bA[:], scalar1=-1.0, scalar2=1.0,
                            op0=OP.mult, op1=OP.add)
            sV.inc(V.tensor_tensor(out=bGK[:], in0=bGK[:], in1=bK[:],
                                   op=OP.mult))
            barrier()
            if dbg:
                G.wait_ge(sV.h, sV.v)
                G.wait_ge(sAC.h, sAC.v)
                for i2, srcb in enumerate((bK, bQ, bV, bA, bGK)):
                    sG.inc(G.dma_start(out=d_bk[i2 * 64:(i2 + 1) * 64, :],
                                       in_=srcb[:]), 16)
                barrier()
            V.memset(bH[:], 0.0)
            sV.inc(V.memset(bYt[:], 0.0))
            barrier()
            SIGSC = float(1.0 / (1.0 + 1e-6))
            for cb in range(L // 256):
                for lt2 in range(2):
                    lt = cb * 2 + lt2
                    for srcf, rdst in ((bV, bVr), (bGK, bGKr)):
                        T.wait_ge(sV.h, sV.v)
                        T.wait_ge(sG.h, sG.v)
                        for g in range(4):
                            sPE.inc(T.transpose(
                                bpt[0:32, 0:64],
                                srcf[:, lt * 128 + g:(lt + 1) * 128:4],
                                sid[0:64, 0:64]))
                            V.wait_ge(sPE.h, sPE.v)
                            V.wait_ge(sG.h, sG.v)
                            sV.inc(V.tensor_copy(
                                bYt[32 * g:32 * g + 32, 0:DC],
                                bpt[0:32, 0:64]))
                            T.wait_ge(sV.h, sV.v)
                        G.wait_ge(sV.h, sV.v)
                        for g in range(4):
                            sG.inc(G.dma_start(
                                out=rdst[32 * g:32 * g + 1,
                                         lt2 * 32 * DC:(lt2 + 1) * 32 * DC],
                                in_=bYt[32 * g:32 * g + 32, 0:DC]), 16)
                barrier()
                for tt in range(256):
                    t = cb * 256 + tt
                    g = 32 * (tt % 4)
                    qo = (tt // 4) * DC
                    T.wait_ge(sV.h, sV.v)
                    T.matmul(bpk1[:], bQ[:, t:t + 1], bH[:], start=True,
                             stop=True)
                    sPE.inc(T.matmul(bpk2[:], bK[:, t:t + 1], bH[:],
                                     start=True, stop=True))
                    V.wait_ge(sPE.h, sPE.v)
                    V.tensor_tensor(out=bwe[0:1, 0:64],
                                    in0=bVr[g:g + 1, qo:qo + DC], in1=bpk1[:],
                                    op=OP.subtract)
                    VS(V.tensor_tensor(out=bwe[0:1, 64:128],
                                       in0=bVr[g:g + 1, qo:qo + DC],
                                       in1=bpk2[:], op=OP.subtract))
                    VS(V.tensor_tensor(out=bsq[:], in0=bwe[0:1, 0:64],
                                       in1=bwe[0:1, 0:64], op=OP.mult))
                    V.tensor_scalar(out=bHd[:], in0=bH[:],
                                    scalar1=bA[:, t:t + 1],
                                    scalar2=None, op0=OP.mult)
                    sV.inc(V.reduce_sum(out=berr[:], in_=bsq[:], axis=AX.X))
                    S.wait_ge(sV.h, sV.v)
                    sAC.inc(S.activation(out=bs[:], in_=berr[:],
                                         func=AF.Sigmoid, scale=SIGSC))
                    V.wait_ge(sAC.h, sAC.v)
                    sV.inc(V.tensor_scalar(out=bu[:],
                                           in0=bGKr[g:g + 1, qo:qo + DC],
                                           scalar1=bs[0:1, 0:1], scalar2=None,
                                           op0=OP.mult))
                    T.wait_ge(sV.h, sV.v)
                    sPE.inc(T.matmul(bpO[:], bu[:], bwe[0:1, 64:128],
                                     start=True, stop=True))
                    V.wait_ge(sPE.h, sPE.v)
                    sV.inc(V.tensor_tensor(out=bH[:], in0=bHd[:], in1=bpO[:],
                                           op=OP.add))
                    T.wait_ge(sV.h, sV.v)
                    sPE.inc(T.matmul(bpY[:], bQ[:, t:t + 1], bH[:], start=True,
                                     stop=True))
                    V.wait_ge(sPE.h, sPE.v)
                    sV.inc(V.tensor_copy(bYr[g:g + 1, qo:qo + DC], bpY[:]))
                barrier()
                for lt2 in range(2):
                    lt = cb * 2 + lt2
                    G.wait_ge(sV.h, sV.v)
                    for g in range(4):
                        sG.inc(G.dma_start(
                            out=bYt[32 * g:32 * g + 32, 0:DC],
                            in_=bYr[32 * g:32 * g + 1,
                                    lt2 * 32 * DC:(lt2 + 1) * 32 * DC]), 16)
                    V.wait_ge(sG.h, sG.v)
                    S.wait_ge(sG.h, sG.v)
                    sAC.inc(S.activation(out=ln_tmp[:, 0:DC], in_=bYt[:, 0:DC],
                                         func=AF.Square,
                                         accum_out=bstat[:, 0:1]))
                    V.wait_ge(sAC.h, sAC.v)
                    sV.inc(V.tensor_scalar(out=bstat[:, 0:1],
                                           in0=bstat[:, 0:1],
                                           scalar1=1.0 / DC, scalar2=1e-6,
                                           op0=OP.mult, op1=OP.add))
                    S.wait_ge(sV.h, sV.v)
                    sAC.inc(S.activation(out=bstat[:, 1:2], in_=bstat[:, 0:1],
                                         func=AF.Sqrt))
                    V.wait_ge(sAC.h, sAC.v)
                    VS(V.reciprocal(out=bstat[:, 1:2], in_=bstat[:, 1:2]))
                    sV.inc(V.tensor_scalar(out=bYt[:, 0:DC], in0=bYt[:, 0:DC],
                                           scalar1=bstat[:, 1:2], scalar2=None,
                                           op0=OP.mult))
                    T.wait_ge(sV.h, sV.v)
                    sPE.inc(T.transpose(bpt[0:64, :], bYt[:, 0:DC], sid[:]))
                    V.wait_ge(sPE.h, sPE.v)
                    V.tensor_scalar(
                        out=bsqf[:, lt * 128:(lt + 1) * 128].rearrange(
                            "c (q four) -> c four q", four=4),
                        in0=bpt[0:64, :], scalar1=sbsm[:, 2:3],
                        scalar2=None, op0=OP.mult)
                    V.tensor_copy(bynh[:, lt * 128:(lt + 1) * 128],
                                  bsqf[:, lt * 128:(lt + 1) * 128])
                    sV.inc(V.tensor_tensor(
                        out=bynl[:, lt * 128:(lt + 1) * 128],
                        in0=bsqf[:, lt * 128:(lt + 1) * 128],
                        in1=bynh[:, lt * 128:(lt + 1) * 128],
                        op=OP.subtract))
                    T.wait_ge(sV.h, sV.v)
                barrier()
            barrier()
            if dbg:
                sG.inc(G.dma_start(out=d_yn, in_=bsqf[:]), 16)
                barrier()
            for lt in range(L // 128):
                SY.wait_ge(sV.h, sV.v)
                SY.wait_ge(sG.h, sG.v)
                sW.inc(SY.dma_start(out=bxt[:, 0:D],
                                    in_=X2s[lt * 128:(lt + 1) * 128, :]), 16)
                V.wait_ge(sW.h, sW.v)
                for dn in range(4):
                    T.matmul(bp0[:], bynh[:, lt * 128:(lt + 1) * 128],
                             sWoT[0][:, dn * 512:(dn + 1) * 512], start=True,
                             stop=False)
                    T.matmul(bp0[:], bynl[:, lt * 128:(lt + 1) * 128],
                             sWoT[0][:, dn * 512:(dn + 1) * 512], start=False,
                             stop=False)
                    sPE.inc(T.matmul(bp0[:], bynh[:, lt * 128:(lt + 1) * 128],
                                     sWoT[1][:, dn * 512:(dn + 1) * 512],
                                     start=False, stop=True))
                    V.wait_ge(sPE.h, sPE.v)
                    sV.inc(V.tensor_tensor(
                        out=bxt[:, dn * 512:(dn + 1) * 512],
                        in0=bxt[:, dn * 512:(dn + 1) * 512], in1=bp0[:],
                        op=OP.add))
                    T.wait_ge(sV.h, sV.v)
                G.wait_ge(sV.h, sV.v)
                dst = out if stop_stage == 3 else X3s
                sG.inc(G.dma_start(out=dst[lt * 128:(lt + 1) * 128, :],
                                   in_=bxt[:, 0:D]), 16)
            barrier()
        barrier()
        if stop_stage == 3:
            G.wait_ge(sG.h, sG.v)
            ctx.close()
            return nc

        ffn_stage(X3s, None, to_out=True)
        G.wait_ge(sG.h, sG.v)
        ctx.close()
        return nc


def _hadamard(n):
    m = np.array([[1.0]], dtype=np.float64)
    while m.shape[0] < n:
        m = np.kron(np.array([[1.0, 1.0], [1.0, -1.0]]), m)
    return m


def kernel(**inputs):
    import ml_dtypes
    L = 2048
    nc = build(L=L)
    consts = {
        "c_ident": np.eye(128, dtype=np.float32),
        "c_h64": np.vstack([_hadamard(64), _hadamard(64)]).astype(ml_dtypes.bfloat16),
        "c_h128": _hadamard(128).astype(ml_dtypes.bfloat16),
        "c_ones": np.ones((128, 128), dtype=np.float32),
    }
    names = {
        "a_W_in": "L0_attn_a_W_in", "a_conv_w": "L0_attn_a_conv_w",
        "a_conv_b": "L0_attn_a_conv_b", "a_W_x": "L0_attn_a_W_x",
        "a_W_dt": "L0_attn_a_W_dt", "a_dt_bias": "L0_attn_a_dt_bias",
        "a_A": "L0_attn_a_A", "a_D": "L0_attn_a_D",
        "a_W_out": "L0_attn_a_W_out",
        "b_Wk": "L0_attn_b_Wk", "b_Wv": "L0_attn_b_Wv", "b_Wq": "L0_attn_b_Wq",
        "b_Wa_w": "L0_attn_b_Wa_w", "b_Wa_b": "L0_attn_b_Wa_b",
        "b_lam": "L0_attn_b_lam", "b_norm_w": "L0_attn_b_norm_w",
        "b_Wo": "L0_attn_b_Wo",
        "f_W1": "L0_ffn_W1", "f_W2": "L0_ffn_W2", "f_W3": "L0_ffn_W3",
    }
    x = np.ascontiguousarray(inputs["L0_attn_a_hidden_states"])
    base = {k: np.ascontiguousarray(inputs[v]) for k, v in names.items()}
    base.update(consts)
    in_maps = []
    for c in range(8):
        m = dict(base)
        m["x"] = np.ascontiguousarray(x[c])
        in_maps.append(m)
    res = run_bass_kernel_spmd(nc, in_maps, core_ids=list(range(8)))
    return np.stack([r["out"] for r in res.results], axis=0)


# version 06 (final)
# speedup vs baseline: 1.3205x; 1.3205x over previous
import contextlib
import numpy as np

import concourse.bass as bass
import concourse.mybir as mybir
from concourse.bass_utils import run_bass_kernel_spmd
from concourse.alu_op_type import AluOpType as OP

F32 = mybir.dt.float32
BF16 = mybir.dt.bfloat16
AF = mybir.ActivationFunctionType
AX = mybir.AxisListType

D = 2048
E = 4096
HF = 8192
NS = 16
DC = 64
DTR = 128
RSQN = float(4096.0 ** -0.5)
RSQH = float(8192.0 ** -0.5)


class Sem:
    def __init__(self, h):
        self.h = h
        self.v = 0

    def inc(self, inst, k=1):
        inst.then_inc(self.h, k)
        self.v += k
        return inst


def build(L=2048, LCA=256, LCF=256, stop_stage=4, dbg=False):
    nc = bass.Bass("TRN2", target_bir_lowering=False,
                   detect_race_conditions=False)

    x_in = nc.dram_tensor("x", [L, D], F32, kind="ExternalInput").ap()
    out = nc.dram_tensor("out", [L, D], F32, kind="ExternalOutput").ap()

    def win(name, shape):
        return nc.dram_tensor(name, shape, F32, kind="ExternalInput").ap()

    W_in = win("a_W_in", [2 * E, D])
    conv_w = win("a_conv_w", [E, 4])
    conv_b = win("a_conv_b", [E])
    W_x = win("a_W_x", [DTR + 2 * NS, E])
    W_dt = win("a_W_dt", [E, DTR])
    dt_bias = win("a_dt_bias", [E])
    A_t = win("a_A", [E, NS])
    D_t = win("a_D", [E])
    W_out = win("a_W_out", [D, E])
    Wk = win("b_Wk", [DC, D])
    Wv = win("b_Wv", [DC, D])
    Wq = win("b_Wq", [DC, D])
    Wa_w = win("b_Wa_w", [DC, D])
    Wa_b = win("b_Wa_b", [DC])
    lam = win("b_lam", [DC])
    norm_w = win("b_norm_w", [DC])
    Wo = win("b_Wo", [D, DC])
    W1 = win("f_W1", [HF, D])
    W2 = win("f_W2", [D, HF])
    W3 = win("f_W3", [HF, D])

    ident = nc.dram_tensor("c_ident", [128, 128], F32, kind="ExternalInput").ap()
    h64c = nc.dram_tensor("c_h64", [128, 64], BF16, kind="ExternalInput").ap()
    h128c = nc.dram_tensor("c_h128", [128, 128], BF16, kind="ExternalInput").ap()
    onesc = nc.dram_tensor("c_ones", [128, 128], F32, kind="ExternalInput").ap()

    def scr(name, shape, dt=BF16):
        return nc.dram_tensor(name, shape, dt, kind="Internal").ap()

    WinT = [scr("s_WinT_h", [2 * E // 128, 128, D // 128, 128]),
            scr("s_WinT_l", [2 * E // 128, 128, D // 128, 128])]
    W1T = [scr("s_W1T_h", [HF // 128, 128, D // 128, 128]),
           scr("s_W1T_l", [HF // 128, 128, D // 128, 128])]
    W3T = [scr("s_W3T_h", [HF // 128, 128, D // 128, 128]),
           scr("s_W3T_l", [HF // 128, 128, D // 128, 128])]
    WxT = [scr("s_WxT_h", [E, 160]), scr("s_WxT_l", [E, 160])]
    WdtT = [scr("s_WdtT_h", [DTR, E]), scr("s_WdtT_l", [DTR, E])]
    WofT = [scr("s_WofT_h", [E, D]), scr("s_WofT_l", [E, D])]
    W2fT = [scr("s_W2fT_h", [HF, D]), scr("s_W2fT_l", [HF, D])]
    WbT = {nm: [scr(f"s_{nm}_h", [D, DC]), scr(f"s_{nm}_l", [D, DC])]
           for nm in ("WkT", "WvT", "WqT", "WaT")}
    WoTs = [scr("s_WoT_h", [DC, D]), scr("s_WoT_l", [DC, D])]
    if dbg:
        d_bk = nc.dram_tensor("d_bk", [5 * 64, L], F32,
                              kind="ExternalOutput").ap()
        d_yn = nc.dram_tensor("d_yn", [64, L], F32,
                              kind="ExternalOutput").ap()
        d_lt = nc.dram_tensor("d_lt", [2 * 128, 2048], F32,
                              kind="ExternalOutput").ap()
        d_st = nc.dram_tensor("d_st", [2 * 128, 8], F32,
                              kind="ExternalOutput").ap()
        d_ln = nc.dram_tensor("d_ln", [16 * 128, 256], F32,
                              kind="ExternalOutput").ap()
        d_xq = nc.dram_tensor("d_xq", [32 * 128, 256], F32,
                              kind="ExternalOutput").ap()
        d_u = nc.dram_tensor("d_u", [32 * 128, 256], F32,
                             kind="ExternalOutput").ap()
        d_dt = nc.dram_tensor("d_dt", [32 * 128, 256], F32,
                              kind="ExternalOutput").ap()
        d_y = nc.dram_tensor("d_y", [32 * 128, 256], F32,
                             kind="ExternalOutput").ap()
    X1s = scr("s_X1", [L, D], F32)
    X2s = scr("s_X2", [L, D], F32)
    X3s = scr("s_X3", [L, D], F32)

    ctx = contextlib.ExitStack()

    def sb(name, shape, dt=F32):
        return ctx.enter_context(nc.sbuf_tensor(name, shape, dt))

    sW = Sem(ctx.enter_context(nc.semaphore("sW")))
    sG = Sem(ctx.enter_context(nc.semaphore("sG")))
    sPE = Sem(ctx.enter_context(nc.semaphore("sPE")))
    sV = Sem(ctx.enter_context(nc.semaphore("sV")))
    sAC = Sem(ctx.enter_context(nc.semaphore("sAC")))
    SEMS = [sW, sG, sPE, sV, sAC]

    T = nc.tensor
    V = nc.vector
    S = nc.scalar
    SY = nc.sync
    G = nc.gpsimd

    def barrier():
        for e in (T, V, S, SY, G):
            for s in SEMS:
                e.wait_ge(s.h, s.v)

    sid = sb("sid", [128, 128], F32)
    sones = sb("sones", [128, 128], F32)
    sH64 = sb("sH64", [128, 64], BF16)
    sH128 = sb("sH128", [128, 128], BF16)
    sA = sb("sA", [128, 32 * NS], F32)
    sdtb = sb("sdtb", [128, 32], F32)
    scvb = sb("scvb", [128, 32], F32)
    scvw = sb("scvw", [128, 32 * 4], F32)
    sD = sb("sD", [128, 32], F32)
    sbsm = sb("sbsm", [64, 8], F32)
    ln_tmp = sb("ln_tmp", [128, D], F32)
    scst = sb("scst", [128, 4], F32)
    for _i, _cv in enumerate((1e-5, 1.0, 1e-8)):
        nc.const_aps.register(F32, float(_cv), scst[:, _i:_i + 1])

    with nc.Block():
        for dst, src in ((sid, ident), (sones, onesc), (sH64, h64c),
                         (sH128, h128c)):
            sW.inc(SY.dma_start(out=dst[:], in_=src), 16)
        sW.inc(SY.dma_start(out=sA.ap().rearrange("p (t n) -> p t n", n=NS),
                            in_=A_t.rearrange("(t p) n -> p t n", p=128)), 16)
        with nc.allow_non_contiguous_dma(reason="tiny one-time vector loads"):
            sW.inc(SY.dma_start(out=sdtb[:],
                                in_=dt_bias.rearrange("(t p) -> p t", p=128)), 16)
            sW.inc(SY.dma_start(out=scvb[:],
                                in_=conv_b.rearrange("(t p) -> p t", p=128)), 16)
        sW.inc(SY.dma_start(out=scvw.ap().rearrange("p (t j) -> p t j", j=4),
                            in_=conv_w.rearrange("(t p) j -> p t j", p=128)), 16)
        with nc.allow_non_contiguous_dma(reason="tiny one-time vector load"):
            sW.inc(SY.dma_start(out=sD[:],
                                in_=D_t.rearrange("(t p) -> p t", p=128)), 16)
        sW.inc(SY.dma_start(out=sbsm[:, 0:1], in_=lam[:, None]), 16)
        sW.inc(SY.dma_start(out=sbsm[:, 1:2], in_=Wa_b[:, None]), 16)
        sW.inc(SY.dma_start(out=sbsm[:, 2:3], in_=norm_w[:, None]), 16)
        for _i, _cv in enumerate((1e-5, 1.0, 1e-8)):
            V.memset(scst[:, _i:_i + 1], float(_cv))
        sV.inc(V.memset(scst[:, 3:4], 0.0))
        barrier()

        with contextlib.ExitStack() as es_p:
            nat = es_p.enter_context(nc.sbuf_tensor("pw_nat", [128, HF], F32))
            tmp = es_p.enter_context(nc.sbuf_tensor("pw_tmp", [128, HF], F32))
            thi = es_p.enter_context(nc.sbuf_tensor("pw_hi", [128, 128], BF16))
            tlo = es_p.enter_context(nc.sbuf_tensor("pw_lo", [128, 128], BF16))
            pps = es_p.enter_context(nc.psum_tensor("pw_ps", [128, 128], F32))

            def split_store(psrc, osz, dma_pairs, scale=1.0, psz=None):
                psz = 128 if psz is None else psz
                if scale != 1.0:
                    S.activation(out=thi[0:psz, 0:osz], in_=psrc, func=AF.Copy,
                                 scale=float(scale))
                    sAC.inc(S.activation(out=ln_tmp[0:psz, 0:osz], in_=psrc,
                                         func=AF.Copy, scale=float(scale)))
                    V.wait_ge(sAC.h, sAC.v)
                    sV.inc(V.tensor_tensor(out=tlo[0:psz, 0:osz],
                                           in0=ln_tmp[0:psz, 0:osz],
                                           in1=thi[0:psz, 0:osz],
                                           op=OP.subtract))
                else:
                    sAC.inc(S.activation(out=thi[0:psz, 0:osz], in_=psrc,
                                         func=AF.Copy))
                    V.wait_ge(sAC.h, sAC.v)
                    sV.inc(V.tensor_tensor(out=tlo[0:psz, 0:osz], in0=psrc,
                                           in1=thi[0:psz, 0:osz],
                                           op=OP.subtract))
                G.wait_ge(sV.h, sV.v)
                for dsta, srca in dma_pairs:
                    sG.inc(G.dma_start(out=dsta, in_=srca), 16)

            def prep_weight(Wsrc, O, I, dst, fold_tile=0, scale=1.0,
                            dst_block=False):
                for ot in range(O // 128):
                    SY.wait_ge(sG.h, sG.v)
                    SY.wait_ge(sPE.h, sPE.v)
                    sW.inc(SY.dma_start(out=nat[:, 0:I],
                                        in_=Wsrc[ot * 128:(ot + 1) * 128, :]), 16)
                    V.wait_ge(sW.h, sW.v)
                    buf = nat
                    if fold_tile:
                        ft = fold_tile
                        nt = I // ft
                        other = tmp
                        for st in range(nt.bit_length() - 1):
                            step = 1 << st
                            npair = nt // (2 * step)
                            a0 = buf.rearrange("p (q two r) -> p q two r",
                                               two=2, r=step * ft)
                            o0 = other.rearrange("p (q two r) -> p q two r",
                                                 two=2, r=step * ft)
                            V.tensor_tensor(out=o0[:, 0:npair, 0, :],
                                            in0=a0[:, 0:npair, 0, :],
                                            in1=a0[:, 0:npair, 1, :], op=OP.add)
                            V.tensor_tensor(out=o0[:, 0:npair, 1, :],
                                            in0=a0[:, 0:npair, 0, :],
                                            in1=a0[:, 0:npair, 1, :],
                                            op=OP.subtract)
                            buf, other = other, buf
                    sV.inc(V.tensor_copy(ln_tmp[:, 0:1], buf[:, 0:1]))
                    T.wait_ge(sV.h, sV.v)
                    for it in range(I // 128):
                        T.wait_ge(sG.h, sG.v)
                        T.wait_ge(sAC.h, sAC.v)
                        sPE.inc(T.transpose(pps[:],
                                            buf[:, it * 128:(it + 1) * 128],
                                            sid[:]))
                        S.wait_ge(sPE.h, sPE.v)
                        V.wait_ge(sPE.h, sPE.v)
                        if dst_block:
                            pairs = [(dst[0][ot, :, it, :], thi[:]),
                                     (dst[1][ot, :, it, :], tlo[:])]
                        else:
                            pairs = [(dst[0][it * 128:(it + 1) * 128,
                                             ot * 128:(ot + 1) * 128], thi[:]),
                                     (dst[1][it * 128:(it + 1) * 128,
                                             ot * 128:(ot + 1) * 128], tlo[:])]
                        split_store(pps[:], 128, pairs, scale)
                barrier()

            prep_weight(W_in, 2 * E, D, WinT, dst_block=True)
            prep_weight(W1, HF, D, W1T, dst_block=True)
            prep_weight(W3, HF, D, W3T, dst_block=True)
            prep_weight(W_dt, E, DTR, WdtT)
            prep_weight(W_out, D, E, WofT, fold_tile=64, scale=RSQN)
            prep_weight(W2, D, HF, W2fT, fold_tile=128, scale=RSQH)
            for o0, osz in ((0, 128), (128, 32)):
                SY.wait_ge(sPE.h, sPE.v)
                SY.wait_ge(sG.h, sG.v)
                sW.inc(SY.dma_start(out=nat[0:osz, 0:E],
                                    in_=W_x[o0:o0 + osz, :]), 16)
                T.wait_ge(sW.h, sW.v)
                for it in range(E // 128):
                    T.wait_ge(sG.h, sG.v)
                    T.wait_ge(sAC.h, sAC.v)
                    sPE.inc(T.transpose(pps[:, 0:osz],
                                        nat[0:osz, it * 128:(it + 1) * 128],
                                        sid[0:osz, 0:osz]))
                    S.wait_ge(sPE.h, sPE.v)
                    V.wait_ge(sPE.h, sPE.v)
                    split_store(pps[:, 0:osz], osz,
                                [(WxT[0][it * 128:(it + 1) * 128, o0:o0 + osz],
                                  thi[:, 0:osz]),
                                 (WxT[1][it * 128:(it + 1) * 128, o0:o0 + osz],
                                  tlo[:, 0:osz])])
                barrier()
            for Wsrc, dsts in ((Wk, WbT["WkT"]), (Wv, WbT["WvT"]),
                               (Wq, WbT["WqT"]), (Wa_w, WbT["WaT"])):
                SY.wait_ge(sPE.h, sPE.v)
                SY.wait_ge(sG.h, sG.v)
                sW.inc(SY.dma_start(out=nat[0:DC, 0:D], in_=Wsrc), 16)
                T.wait_ge(sW.h, sW.v)
                for it in range(16):
                    T.wait_ge(sG.h, sG.v)
                    sPE.inc(T.transpose(pps[:, 0:DC],
                                        nat[0:DC, it * 128:(it + 1) * 128],
                                        sid[0:DC, 0:DC]))
                    S.wait_ge(sPE.h, sPE.v)
                    V.wait_ge(sPE.h, sPE.v)
                    split_store(pps[:, 0:DC], DC,
                                [(dsts[0][it * 128:(it + 1) * 128, :],
                                  thi[:, 0:DC]),
                                 (dsts[1][it * 128:(it + 1) * 128, :],
                                  tlo[:, 0:DC])])
                barrier()
            for it in range(16):
                SY.wait_ge(sPE.h, sPE.v)
                SY.wait_ge(sG.h, sG.v)
                sW.inc(SY.dma_start(out=nat[:, 0:DC],
                                    in_=Wo[it * 128:(it + 1) * 128, :]), 16)
                T.wait_ge(sW.h, sW.v)
                T.wait_ge(sG.h, sG.v)
                sPE.inc(T.transpose(pps[0:DC, :], nat[:, 0:DC], sid[:]))
                S.wait_ge(sPE.h, sPE.v)
                V.wait_ge(sPE.h, sPE.v)
                split_store(pps[0:DC, :], 128,
                            [(WoTs[0][:, it * 128:(it + 1) * 128],
                              thi[0:DC, 0:128]),
                             (WoTs[1][:, it * 128:(it + 1) * 128],
                              tlo[0:DC, 0:128])], psz=DC)
            barrier()
        barrier()

        def VS(inst):
            sV.inc(inst)
            V.wait_ge(sV.h, sV.v)
            return inst

        def SS(inst):
            sAC.inc(inst)
            S.wait_ge(sAC.h, sAC.v)
            return inst

        def ln_fm(src_tok, lnh, lnl, stat, pps2, nlt, ncols, dump=None):
            for j in range(nlt):
                xs = src_tok[:, j * D:(j + 1) * D]
                VS(V.reduce_sum(out=stat[:, 0:1], in_=xs, axis=AX.X))
                sAC.inc(S.activation(out=ln_tmp[:, 0:D], in_=xs, func=AF.Square,
                                     accum_out=stat[:, 1:2]))
                V.wait_ge(sAC.h, sAC.v)
                VS(V.tensor_scalar(out=stat[:, 0:1], in0=stat[:, 0:1],
                                   scalar1=1.0 / D, scalar2=None, op0=OP.mult))
                VS(V.tensor_tensor(out=stat[:, 2:3], in0=stat[:, 0:1],
                                   in1=stat[:, 0:1], op=OP.mult))
                sV.inc(V.scalar_tensor_tensor(out=stat[:, 3:4], in0=stat[:, 1:2],
                                              scalar=1.0 / D, in1=stat[:, 2:3],
                                              op0=OP.mult, op1=OP.subtract))
                S.wait_ge(sV.h, sV.v)
                sAC.inc(S.activation(out=stat[:, 3:4], in_=stat[:, 3:4],
                                     func=AF.Sqrt, bias=1e-5))
                V.wait_ge(sAC.h, sAC.v)
                VS(V.reciprocal(out=stat[:, 3:4], in_=stat[:, 3:4]))
                sV.inc(V.tensor_scalar(out=ln_tmp[:, 0:D], in0=xs,
                                       scalar1=stat[:, 0:1], scalar2=stat[:, 3:4],
                                       op0=OP.subtract, op1=OP.mult))
                T.wait_ge(sV.h, sV.v)
                if dump is not None:
                    G.wait_ge(sV.h, sV.v)
                    sG.inc(G.dma_start(out=dump[j * 128:(j + 1) * 128, :],
                                       in_=ln_tmp[:, 0:D]), 16)
                    sG.inc(G.dma_start(out=d_st[j * 128:(j + 1) * 128, :],
                                       in_=stat[:, 0:8]), 16)
                    for _e in (T, V, S, SY, G):
                        _e.wait_ge(sG.h, sG.v)
                for t in range(D // 128):
                    T.wait_ge(sV.h, sV.v)
                    sPE.inc(T.transpose(pps2[:], ln_tmp[:, t * 128:(t + 1) * 128],
                                        sid[:]))
                    S.wait_ge(sPE.h, sPE.v)
                    co = t * ncols + j * 128
                    sAC.inc(S.activation(out=lnh[:, co:co + 128], in_=pps2[:],
                                         func=AF.Copy))
                    V.wait_ge(sAC.h, sAC.v)
                    sV.inc(V.tensor_tensor(out=lnl[:, co:co + 128], in0=pps2[:],
                                           in1=lnh[:, co:co + 128],
                                           op=OP.subtract))

        def mm3(pst, lh, ll, rh, rl, first, last):
            T.matmul(pst, lh, rh, start=first, stop=False)
            T.matmul(pst, lh, rl, start=False, stop=False)
            return T.matmul(pst, ll, rh, start=False, stop=last)

        def quant_ints(psrc, idst, S_, V_, bscr):
            n = idst.shape[1]
            w = szt_q[:, 0:n]
            b = bscr[:, 0:n]
            sAC.inc(S_.activation(out=idst, in_=psrc, func=AF.Copy,
                                  scale=20.0, bias=130.5))
            V_.wait_ge(sAC.h, sAC.v)
            V_.tensor_scalar(out=idst, in0=idst, scalar1=1.0, scalar2=259.0,
                             op0=OP.max, op1=OP.min)
            V_.tensor_copy(w, idst)
            for c in (256.0, 128.0, 64.0, 32.0, 16.0, 8.0, 4.0, 2.0, 1.0):
                V_.tensor_scalar(out=b, in0=w, scalar1=c, scalar2=None,
                                 op0=OP.is_ge)
                V_.scalar_tensor_tensor(out=w, in0=b, scalar=-c, in1=w,
                                        op0=OP.mult, op1=OP.add)
            V_.tensor_tensor(out=idst, in0=idst, in1=w, op=OP.subtract)
            V_.tensor_scalar(out=idst, in0=idst, scalar1=-130.0,
                             scalar2=-128.0, op0=OP.add, op1=OP.max)
            sV.inc(V_.tensor_scalar(out=idst, in0=idst, scalar1=127.0,
                                    scalar2=None, op0=OP.min))

        NLT = LCA // 128
        szt_q = sb("szt_q", [128, LCA], F32)
        with contextlib.ExitStack() as es_a:
            xt = es_a.enter_context(nc.sbuf_tensor("a_xt", [128, NLT * D], F32))
            xq = es_a.enter_context(
                nc.sbuf_tensor("a_xq", [128, 32 * (LCA + 4)], F32))
            stat = es_a.enter_context(nc.sbuf_tensor("a_stat", [128, 8], F32))
            lnh = es_a.enter_context(nc.sbuf_tensor("a_lnh", [128, 16 * LCA], BF16))
            lnl = es_a.enter_context(nc.sbuf_tensor("a_lnl", [128, 16 * LCA], BF16))
            u = es_a.enter_context(nc.sbuf_tensor("a_u", [128, 32 * LCA], F32))
            ui = es_a.enter_context(nc.sbuf_tensor("a_ui", [128, 32 * LCA], BF16))
            dtf = es_a.enter_context(nc.sbuf_tensor("a_dt", [128, 32 * LCA], F32))
            dtih = es_a.enter_context(nc.sbuf_tensor("a_dtih", [128, LCA], BF16))
            dtil = es_a.enter_context(nc.sbuf_tensor("a_dtil", [128, LCA], BF16))
            bc = es_a.enter_context(nc.sbuf_tensor("a_bc", [32, LCA], F32))
            browA = es_a.enter_context(
                nc.sbuf_tensor("a_brow", [1, 2 * LCA], F32))
            state = es_a.enter_context(nc.sbuf_tensor("a_state", [128, 32 * NS], F32))
            dA = es_a.enter_context(nc.sbuf_tensor("a_dA", [128, LCA], F32))
            pbb = es_a.enter_context(nc.sbuf_tensor("a_pb", [128, LCA], F32))
            hbf = es_a.enter_context(nc.sbuf_tensor("a_h", [128, LCA], F32))
            hcc = es_a.enter_context(nc.sbuf_tensor("a_hc", [128, LCA], F32))
            szt = es_a.enter_context(nc.sbuf_tensor("a_sz", [128, LCA], F32))
            z1h = es_a.enter_context(nc.sbuf_tensor("a_z1h", [128, LCA], BF16))
            z1l = es_a.enter_context(nc.sbuf_tensor("a_z1l", [128, LCA], BF16))
            wsh = es_a.enter_context(nc.sbuf_tensor("a_wsh", [128, 5120], BF16))
            wsl = es_a.enter_context(nc.sbuf_tensor("a_wsl", [128, 5120], BF16))
            wmh = es_a.enter_context(nc.sbuf_tensor("a_wmh", [128, 1024], BF16))
            wml = es_a.enter_context(nc.sbuf_tensor("a_wml", [128, 1024], BF16))
            p0 = es_a.enter_context(nc.psum_tensor("a_p0", [128, LCA], F32))
            p1 = es_a.enter_context(nc.psum_tensor("a_p1", [128, LCA], F32))
            pt2 = es_a.enter_context(nc.psum_tensor("a_pt", [128, 128], F32))
            po0 = es_a.enter_context(nc.psum_tensor("a_po0", [128, 512], F32))
            po1 = es_a.enter_context(nc.psum_tensor("a_po1", [128, 512], F32))

            V.memset(state[:], 0.0)
            sV.inc(V.memset(xq[:], 0.0))
            barrier()

            def xqs(t):
                return xq[:, t * (LCA + 4) + 4:(t + 1) * (LCA + 4)]

            yvs = xqs

            for c in range(L // LCA):
                SY.wait_ge(sV.h, sV.v)
                SY.wait_ge(sG.h, sG.v)
                for j in range(NLT):
                    sW.inc(SY.dma_start(
                        out=xt[:, j * D:(j + 1) * D],
                        in_=x_in[c * LCA + j * 128:c * LCA + (j + 1) * 128, :]),
                        16)
                V.wait_ge(sW.h, sW.v)
                S.wait_ge(sW.h, sW.v)
                ln_fm(xt, lnh, lnl, stat, pt2, NLT, LCA,
                      dump=(d_lt if (dbg and c == 0) else None))
                barrier()
                if dbg and c == 0:
                    for t in range(16):
                        V.tensor_tensor(out=szt[:], in0=lnh[:, t * LCA:(t + 1) * LCA],
                                        in1=lnl[:, t * LCA:(t + 1) * LCA], op=OP.add)
                        sV.inc(V.tensor_copy(hcc[:], szt[:]))
                        G.wait_ge(sV.h, sV.v)
                        sG.inc(G.dma_start(
                            out=d_ln[t * 128:(t + 1) * 128, :],
                            in_=hcc[:]), 16)
                    barrier()
                pe_hist = {}
                wv_hist = {}

                def _pref_m(m):
                        par = (m % 2) * 2048
                        sW.inc(SY.dma_start(out=wsh[:, par:par + 2048],
                                            in_=WinT[0][m]), 16)
                        sW.inc(SY.dma_start(out=wsl[:, par:par + 2048],
                                            in_=WinT[1][m]), 16)
                        wv_hist[m] = sW.v

                SY.wait_ge(sPE.h, sPE.v)
                _pref_m(0)
                _pref_m(1)
                for m in range(2, 34):
                    mm = m - 2
                    par = (mm % 2) * 2048
                    T.wait_ge(sW.h, wv_hist[mm])
                    T.wait_ge(sAC.h, sAC.v)
                    for k in range(16):
                        mi = mm3(p0[:], wsh[:, par + k * 128:par + (k + 1) * 128],
                                 wsl[:, par + k * 128:par + (k + 1) * 128],
                                 lnh[:, k * LCA:(k + 1) * LCA],
                                 lnl[:, k * LCA:(k + 1) * LCA], k == 0, k == 15)
                    sPE.inc(mi)
                    pe_hist[mm] = sPE.v
                    if mm + 2 < 32:
                        SY.wait_ge(sPE.h, pe_hist[mm])
                        _pref_m(mm + 2)
                    S.wait_ge(sPE.h, sPE.v)
                    quant_ints(p0[:], xqs(mm), S, V, hbf)
                for t in range(32):
                    xx = xq[:, t * (LCA + 4):(t + 1) * (LCA + 4)]
                    V.wait_ge(sAC.h, sAC.v)
                    V.tensor_scalar(out=dA[:], in0=xx[:, 1:1 + LCA],
                                    scalar1=scvw[:, 4 * t:4 * t + 1],
                                    scalar2=None, op0=OP.mult)
                    for j in (1, 2, 3):
                        V.scalar_tensor_tensor(
                            out=dA[:], in0=xx[:, 1 + j:1 + j + LCA],
                            scalar=scvw[:, 4 * t + j:4 * t + j + 1],
                            in1=dA[:], op0=OP.mult, op1=OP.add)
                    sV.inc(V.tensor_copy(xx[:, 1:4], xx[:, LCA + 1:LCA + 4]))
                    S.wait_ge(sV.h, sV.v)
                    sAC.inc(S.activation(out=pbb[:], in_=dA[:], func=AF.Silu,
                                         scale=0.05, bias=scvb[:, t:t + 1]))
                    S.wait_ge(sV.h, sV.v)
                    quant_ints(pbb[:], hcc[:], S, V, hbf)
                    V.tensor_scalar(out=u[:, t * LCA:(t + 1) * LCA], in0=hcc[:],
                                    scalar1=0.05, scalar2=None, op0=OP.mult)
                    sV.inc(V.tensor_copy(ui[:, t * LCA:(t + 1) * LCA], hcc[:]))
                if dbg and c == 0:
                    for t in range(32):
                        sV.inc(V.tensor_copy(hcc[:], xqs(t)))
                        G.wait_ge(sV.h, sV.v)
                        sG.inc(G.dma_start(
                            out=d_xq[t * 128:(t + 1) * 128, :],
                            in_=hcc[:]), 16)
                        sV.inc(V.tensor_copy(hcc[:], u[:, t * LCA:(t + 1) * LCA]))
                        G.wait_ge(sV.h, sV.v)
                        sG.inc(G.dma_start(
                            out=d_u[t * 128:(t + 1) * 128, :],
                            in_=hcc[:]), 16)
                    barrier()
                SY.wait_ge(sPE.h, sPE.v)
                sW.inc(SY.dma_start(
                    out=wsh.ap()[:, 0:32 * 160].rearrange(
                        "p (k o) -> p k o", o=160),
                    in_=WxT[0].rearrange("(k p) o -> p k o", p=128)), 16)
                sW.inc(SY.dma_start(
                    out=wsl.ap()[:, 0:32 * 160].rearrange(
                        "p (k o) -> p k o", o=160),
                    in_=WxT[1].rearrange("(k p) o -> p k o", p=128)), 16)
                T.wait_ge(sW.h, sW.v)
                T.wait_ge(sV.h, sV.v)
                T.wait_ge(sAC.h, sAC.v)
                for m0, msz, pdst in ((0, 128, p0), (128, 32, p1)):
                    for k in range(32):
                        lh = wsh[:, k * 160 + m0:k * 160 + m0 + msz]
                        ll = wsl[:, k * 160 + m0:k * 160 + m0 + msz]
                        rh = ui[:, k * LCA:(k + 1) * LCA]
                        T.matmul(pdst[0:msz, :], lh, rh, start=(k == 0),
                                 stop=False)
                        mi = T.matmul(pdst[0:msz, :], ll, rh, start=False,
                                      stop=(k == 31))
                sPE.inc(mi)
                S.wait_ge(sPE.h, sPE.v)
                sAC.inc(S.activation(out=dtih[:], in_=p0[:], func=AF.Copy,
                                     scale=0.05))
                sAC.inc(S.activation(out=bc[:], in_=p1[0:32, :], func=AF.Copy,
                                     scale=0.05))
                V.wait_ge(sAC.h, sAC.v)
                sV.inc(V.scalar_tensor_tensor(out=dtil[:], in0=p0[:],
                                              scalar=0.05, in1=dtih[:],
                                              op0=OP.mult, op1=OP.subtract))
                SY.wait_ge(sPE.h, sPE.v)
                sW.inc(SY.dma_start(out=wsh[:, 0:E], in_=WdtT[0]), 16)
                sW.inc(SY.dma_start(out=wsl[:, 0:E], in_=WdtT[1]), 16)
                T.wait_ge(sW.h, sW.v)
                T.wait_ge(sV.h, sV.v)
                for m in range(32):
                    sPE.inc(mm3(p0[:], wsh[:, m * 128:(m + 1) * 128],
                                wsl[:, m * 128:(m + 1) * 128], dtih[:], dtil[:],
                                True, True))
                    S.wait_ge(sPE.h, sPE.v)
                    S.activation(out=szt[:], in_=p0[:], func=AF.Exp,
                                 bias=sdtb[:, m:m + 1])
                    sAC.inc(S.activation(out=dtf[:, m * LCA:(m + 1) * LCA],
                                         in_=szt[:], func=AF.Ln, bias=1.0))
                    T.wait_ge(sAC.h, sAC.v)
                if dbg and c == 0:
                    for t in range(32):
                        sV.inc(V.tensor_copy(hcc[:], dtf[:, t * LCA:(t + 1) * LCA]))
                        G.wait_ge(sV.h, sV.v)
                        sG.inc(G.dma_start(
                            out=d_dt[t * 128:(t + 1) * 128, :],
                            in_=hcc[:]), 16)
                    barrier()
                V.wait_ge(sAC.h, sAC.v)
                for t in range(32):
                    V.tensor_scalar(out=yvs(t), in0=u[:, t * LCA:(t + 1) * LCA],
                                    scalar1=sD[:, t:t + 1], scalar2=None,
                                    op0=OP.mult)
                for t in range(32):
                    V.tensor_tensor(out=u[:, t * LCA:(t + 1) * LCA],
                                    in0=dtf[:, t * LCA:(t + 1) * LCA],
                                    in1=u[:, t * LCA:(t + 1) * LCA], op=OP.mult)
                sV.inc(V.tensor_copy(hcc[:, 0:1], u[:, 0:1]))
                for n in range(NS):
                    T.wait_ge(sV.h, sV.v)
                    sPE.inc(T.matmul(p1[0:1, :], sid[0:32, n:n + 1], bc[:],
                                     start=True, stop=True))
                    V.wait_ge(sPE.h, sPE.v)
                    sV.inc(V.tensor_copy(browA[0:1, 0:LCA], p1[0:1, :]))
                    T.wait_ge(sV.h, sV.v)
                    T.matmul(po0[:, 0:LCA], sones[0:1, 0:128],
                             browA[0:1, 0:LCA], start=True, stop=True)
                    sPE.inc(T.matmul(p1[0:1, :], sid[0:32, 16 + n:17 + n],
                                     bc[:], start=True, stop=True))
                    V.wait_ge(sPE.h, sPE.v)
                    sV.inc(V.tensor_copy(browA[0:1, LCA:2 * LCA], p1[0:1, :]))
                    T.wait_ge(sV.h, sV.v)
                    sPE.inc(T.matmul(po1[:, 0:LCA], sones[0:1, 0:128],
                                     browA[0:1, LCA:2 * LCA], start=True,
                                     stop=True))
                    S.wait_ge(sPE.h, sPE.v)
                    V.wait_ge(sPE.h, sPE.v)
                    for t in range(32):
                        S.wait_ge(sV.h, sV.v)
                        sAC.inc(S.activation(
                            out=dA[:], in_=dtf[:, t * LCA:(t + 1) * LCA],
                            func=AF.Exp,
                            scale=sA[:, t * NS + n:t * NS + n + 1]))
                        V.tensor_tensor(out=pbb[:],
                                        in0=u[:, t * LCA:(t + 1) * LCA],
                                        in1=po0[:, 0:LCA], op=OP.mult)
                        V.wait_ge(sAC.h, sAC.v)
                        VS(V.tensor_tensor_scan(
                            out=hbf[:], data0=dA[:], data1=pbb[:],
                            initial=state[:, t * NS + n:t * NS + n + 1],
                            op0=OP.mult, op1=OP.add))
                        V.tensor_copy(state[:, t * NS + n:t * NS + n + 1],
                                      hbf[:, LCA - 1:LCA])
                        V.tensor_tensor(out=hcc[:], in0=hbf[:], in1=po1[:, 0:LCA],
                                        op=OP.mult)
                        sV.inc(V.tensor_tensor(out=yvs(t), in0=yvs(t),
                                               in1=hcc[:], op=OP.add))
                pe_hist = {}
                wv_hist = {}

                def _pref_m(m):
                        par = (m % 2) * 2048
                        sW.inc(SY.dma_start(out=wsh[:, par:par + 2048],
                                            in_=WinT[0][32 + m]), 16)
                        sW.inc(SY.dma_start(out=wsl[:, par:par + 2048],
                                            in_=WinT[1][32 + m]), 16)
                        wv_hist[m] = sW.v

                SY.wait_ge(sPE.h, sPE.v)
                _pref_m(0)
                _pref_m(1)
                for m in range(2, 34):
                    mm = m - 2
                    par = (mm % 2) * 2048
                    T.wait_ge(sW.h, wv_hist[mm])
                    T.wait_ge(sAC.h, sAC.v)
                    for k in range(16):
                        mi = mm3(p0[:], wsh[:, par + k * 128:par + (k + 1) * 128],
                                 wsl[:, par + k * 128:par + (k + 1) * 128],
                                 lnh[:, k * LCA:(k + 1) * LCA],
                                 lnl[:, k * LCA:(k + 1) * LCA], k == 0, k == 15)
                    sPE.inc(mi)
                    pe_hist[mm] = sPE.v
                    if mm + 2 < 32:
                        SY.wait_ge(sPE.h, pe_hist[mm])
                        _pref_m(mm + 2)
                    S.wait_ge(sPE.h, sPE.v)
                    sAC.inc(S.activation(out=szt[:], in_=p0[:], func=AF.Silu))
                    V.wait_ge(sAC.h, sAC.v)
                    sV.inc(V.tensor_tensor(out=yvs(mm), in0=yvs(mm), in1=szt[:],
                                           op=OP.mult))
                if dbg and c == 0:
                    for t in range(32):
                        sV.inc(V.tensor_copy(hcc[:], yvs(t)))
                        G.wait_ge(sV.h, sV.v)
                        sG.inc(G.dma_start(
                            out=d_y[t * 128:(t + 1) * 128, :],
                            in_=hcc[:]), 16)
                    barrier()
                for t in range(32):
                    V.tensor_copy(ui[:, t * LCA:(t + 1) * LCA], yvs(t))
                for t in range(32):
                    dst = lnh if t < 16 else lnl
                    V.tensor_tensor(out=dst[:, (t % 16) * LCA:(t % 16 + 1) * LCA],
                                    in0=yvs(t),
                                    in1=ui[:, t * LCA:(t + 1) * LCA],
                                    op=OP.subtract)
                sV.inc(V.tensor_copy(hcc[:, 0:1], ui[:, 0:1]))

                def yh_t(t):
                    return ui[:, t * LCA:(t + 1) * LCA]

                def yl_t(t):
                    dst = lnh if t < 16 else lnl
                    return dst[:, (t % 16) * LCA:(t % 16 + 1) * LCA]

                T.wait_ge(sV.h, sV.v)
                for dn in range(4):
                    pe_hist = {}
                    wv_hist = {}

                    def _pref_kk(kk):
                            par = (kk % 2) * 512
                            sW.inc(SY.dma_start(
                                out=wmh[:, par:par + 512],
                                in_=WofT[0][kk * 128:(kk + 1) * 128,
                                            dn * 512:(dn + 1) * 512]), 16)
                            sW.inc(SY.dma_start(
                                out=wml[:, par:par + 512],
                                in_=WofT[1][kk * 128:(kk + 1) * 128,
                                            dn * 512:(dn + 1) * 512]), 16)
                            wv_hist[kk] = sW.v

                    SY.wait_ge(sPE.h, sPE.v)
                    _pref_kk(0)
                    _pref_kk(1)
                    for kk in range(2, 34):
                        k = kk - 2
                        par = (k % 2) * 512
                        T.wait_ge(sAC.h, sAC.v)
                        T.wait_ge(sV.h, sV.v)
                        T.matmul(p0[0:64, :], sH64[0:64, :], yh_t(k)[0:64, :],
                                 start=True, stop=False)
                        T.matmul(p0[0:64, :], sH64[0:64, :], yl_t(k)[0:64, :],
                                 start=False, stop=True)
                        T.matmul(p0[64:128, :], sH64[64:128, :],
                                 yh_t(k)[64:128, :],
                                 start=True, stop=False, tile_position=(64, 64))
                        sPE.inc(T.matmul(p0[64:128, :], sH64[64:128, :],
                                         yl_t(k)[64:128, :], start=False,
                                         stop=True, tile_position=(64, 64)))
                        S.wait_ge(sPE.h, sPE.v)
                        sAC.inc(S.activation(out=z1h[:], in_=p0[:], func=AF.Copy))
                        V.wait_ge(sAC.h, sAC.v)
                        sV.inc(V.tensor_tensor(out=z1l[:], in0=p0[:], in1=z1h[:],
                                               op=OP.subtract))
                        T.wait_ge(sW.h, wv_hist[k])
                        T.wait_ge(sV.h, sV.v)
                        for lm in range(NLT):
                            pd = po0 if lm == 0 else po1
                            zh = z1h[:, lm * 128:(lm + 1) * 128]
                            zl = z1l[:, lm * 128:(lm + 1) * 128]
                            T.matmul(pd[:], zh, wmh[:, par:par + 512],
                                     start=(k == 0), stop=False)
                            T.matmul(pd[:], zh, wml[:, par:par + 512],
                                     start=False, stop=False)
                            mi = T.matmul(pd[:], zl, wmh[:, par:par + 512],
                                          start=False, stop=(k == 31))
                        sPE.inc(mi)
                        pe_hist[k] = sPE.v
                        if k + 2 < 32:
                            SY.wait_ge(sPE.h, pe_hist[k])
                            _pref_kk(k + 2)
                    V.wait_ge(sPE.h, sPE.v)
                    for lm in range(NLT):
                        pd = po0 if lm == 0 else po1
                        sl = slice(lm * D + dn * 512, lm * D + (dn + 1) * 512)
                        sV.inc(V.tensor_tensor(out=xt[:, sl], in0=xt[:, sl],
                                               in1=pd[:], op=OP.add))
                    T.wait_ge(sV.h, sV.v)
                G.wait_ge(sV.h, sV.v)
                for j in range(NLT):
                    sG.inc(G.dma_start(
                        out=X1s[c * LCA + j * 128:c * LCA + (j + 1) * 128, :],
                        in_=xt[:, j * D:(j + 1) * D]), 16)
                barrier()
            barrier()
        barrier()

        def scratch_to_out(src):
            with nc.sbuf_tensor("cp_t", [128, D], F32) as cpt:
                for c in range(L // 128):
                    SY.wait_ge(sG.h, sG.v)
                    sW.inc(SY.dma_start(out=cpt[:],
                                        in_=src[c * 128:(c + 1) * 128, :]), 16)
                    G.wait_ge(sW.h, sW.v)
                    sG.inc(G.dma_start(out=out[c * 128:(c + 1) * 128, :],
                                       in_=cpt[:]), 16)
                G.wait_ge(sG.h, sG.v)
                barrier()

        if stop_stage == 1:
            scratch_to_out(X1s)
            ctx.close()
            return nc

        _ffn_n = [0]

        def ffn_stage(Xin_s, Xout_s, to_out):
            NLF = LCF // 128
            fx = f"g{_ffn_n[0]}_"
            _ffn_n[0] += 1
            with contextlib.ExitStack() as es_f:
                fxt = es_f.enter_context(nc.sbuf_tensor(fx + "f_xt", [128, NLF * D], F32))
                fstat = es_f.enter_context(nc.sbuf_tensor(fx + "f_stat", [128, 8], F32))
                flnh = es_f.enter_context(nc.sbuf_tensor(fx + "f_lnh", [128, 16 * LCF], BF16))
                flnl = es_f.enter_context(nc.sbuf_tensor(fx + "f_lnl", [128, 16 * LCF], BF16))
                fz1h = es_f.enter_context(nc.sbuf_tensor(fx + "f_z1h", [128, 64 * LCF], BF16))
                fz1l = es_f.enter_context(nc.sbuf_tensor(fx + "f_z1l", [128, 64 * LCF], BF16))
                fhh = es_f.enter_context(nc.sbuf_tensor(fx + "f_hh", [128, LCF], BF16))
                fhl = es_f.enter_context(nc.sbuf_tensor(fx + "f_hl", [128, LCF], BF16))
                ft0 = es_f.enter_context(nc.sbuf_tensor(fx + "f_t0", [128, LCF], F32))
                ft1 = es_f.enter_context(nc.sbuf_tensor(fx + "f_t1", [128, LCF], F32))
                fw1h = es_f.enter_context(nc.sbuf_tensor(fx + "f_w1h", [128, 2 * 16 * 128], BF16))
                fw1l = es_f.enter_context(nc.sbuf_tensor(fx + "f_w1l", [128, 2 * 16 * 128], BF16))
                fw3h = es_f.enter_context(nc.sbuf_tensor(fx + "f_w3h", [128, 2 * 16 * 128], BF16))
                fw3l = es_f.enter_context(nc.sbuf_tensor(fx + "f_w3l", [128, 2 * 16 * 128], BF16))
                fwmh = es_f.enter_context(nc.sbuf_tensor(fx + "f_wmh", [128, 2 * 512], BF16))
                fwml = es_f.enter_context(nc.sbuf_tensor(fx + "f_wml", [128, 2 * 512], BF16))
                fp0 = es_f.enter_context(nc.psum_tensor(fx + "f_p0", [128, LCF], F32))
                fp1 = es_f.enter_context(nc.psum_tensor(fx + "f_p1", [128, LCF], F32))
                fpt = es_f.enter_context(nc.psum_tensor(fx + "f_pt", [128, 128], F32))
                fpo0 = es_f.enter_context(nc.psum_tensor(fx + "f_po0", [128, 512], F32))
                fpo1 = es_f.enter_context(nc.psum_tensor(fx + "f_po1", [128, 512], F32))
                for c in range(L // LCF):
                    SY.wait_ge(sG.h, sG.v)
                    for j in range(NLF):
                        sW.inc(SY.dma_start(
                            out=fxt[:, j * D:(j + 1) * D],
                            in_=Xin_s[c * LCF + j * 128:
                                      c * LCF + (j + 1) * 128, :]), 16)
                    V.wait_ge(sW.h, sW.v)
                    S.wait_ge(sW.h, sW.v)
                    ln_fm(fxt, flnh, flnl, fstat, fpt, NLF, LCF)
                    barrier()
                    pe_hist = {}
                    wv_hist = {}

                    def _pref_m(m):
                            par = (m % 2) * 2048
                            sW.inc(SY.dma_start(out=fw1h[:, par:par + 2048],
                                                in_=W1T[0][m]), 16)
                            sW.inc(SY.dma_start(out=fw1l[:, par:par + 2048],
                                                in_=W1T[1][m]), 16)
                            sW.inc(SY.dma_start(out=fw3h[:, par:par + 2048],
                                                in_=W3T[0][m]), 16)
                            sW.inc(SY.dma_start(out=fw3l[:, par:par + 2048],
                                                in_=W3T[1][m]), 16)
                            wv_hist[m] = sW.v

                    SY.wait_ge(sPE.h, sPE.v)
                    _pref_m(0)
                    _pref_m(1)
                    for m in range(2, 66):
                        mm = m - 2
                        par = (mm % 2) * 2048
                        m = mm
                        T.wait_ge(sW.h, wv_hist[mm])
                        T.wait_ge(sV.h, sV.v)
                        for k in range(16):
                            mm3(fp0[:], fw1h[:, par + k * 128:par + (k + 1) * 128],
                                fw1l[:, par + k * 128:par + (k + 1) * 128],
                                flnh[:, k * LCF:(k + 1) * LCF],
                                flnl[:, k * LCF:(k + 1) * LCF], k == 0, k == 15)
                        for k in range(16):
                            mi = mm3(fp1[:], fw3h[:, par + k * 128:par + (k + 1) * 128],
                                     fw3l[:, par + k * 128:par + (k + 1) * 128],
                                     flnh[:, k * LCF:(k + 1) * LCF],
                                     flnl[:, k * LCF:(k + 1) * LCF],
                                     k == 0, k == 15)
                        sPE.inc(mi)
                        pe_hist[mm] = sPE.v
                        if mm + 2 < 64:
                            SY.wait_ge(sPE.h, pe_hist[mm])
                            _pref_m(mm + 2)
                        S.wait_ge(sPE.h, sPE.v)
                        sAC.inc(S.activation(out=ft0[:], in_=fp0[:],
                                             func=AF.Silu))
                        V.wait_ge(sAC.h, sAC.v)
                        V.tensor_tensor(out=ft1[:], in0=ft0[:], in1=fp1[:],
                                        op=OP.mult)
                        V.tensor_copy(fhh[:], ft1[:])
                        sV.inc(V.tensor_tensor(out=fhl[:], in0=ft1[:],
                                               in1=fhh[:], op=OP.subtract))
                        T.wait_ge(sV.h, sV.v)
                        T.matmul(fp0[:], sH128[:], fhh[:], start=True,
                                 stop=False)
                        sPE.inc(T.matmul(fp0[:], sH128[:], fhl[:], start=False,
                                         stop=True))
                        S.wait_ge(sPE.h, sPE.v)
                        sAC.inc(S.activation(out=fz1h[:, m * LCF:(m + 1) * LCF],
                                             in_=fp0[:], func=AF.Copy))
                        V.wait_ge(sAC.h, sAC.v)
                        sV.inc(V.tensor_tensor(
                            out=fz1l[:, m * LCF:(m + 1) * LCF], in0=fp0[:],
                            in1=fz1h[:, m * LCF:(m + 1) * LCF],
                            op=OP.subtract))
                        T.wait_ge(sV.h, sV.v)
                    for dn in range(4):
                        pe_hist = {}
                        wv_hist = {}

                        def _pref_kk(kk):
                                par = (kk % 2) * 512
                                sW.inc(SY.dma_start(
                                    out=fwmh[:, par:par + 512],
                                    in_=W2fT[0][kk * 128:(kk + 1) * 128,
                                                dn * 512:(dn + 1) * 512]), 16)
                                sW.inc(SY.dma_start(
                                    out=fwml[:, par:par + 512],
                                    in_=W2fT[1][kk * 128:(kk + 1) * 128,
                                                dn * 512:(dn + 1) * 512]), 16)
                                wv_hist[kk] = sW.v

                        SY.wait_ge(sPE.h, sPE.v)
                        _pref_kk(0)
                        _pref_kk(1)
                        for kk in range(2, 66):
                            k = kk - 2
                            par = (k % 2) * 512
                            T.wait_ge(sW.h, wv_hist[k])
                            for lm in range(NLF):
                                pd = fpo0 if lm == 0 else fpo1
                                zh = fz1h[:, k * LCF + lm * 128:
                                          k * LCF + (lm + 1) * 128]
                                zl = fz1l[:, k * LCF + lm * 128:
                                          k * LCF + (lm + 1) * 128]
                                T.matmul(pd[:], zh, fwmh[:, par:par + 512],
                                         start=(k == 0), stop=False)
                                T.matmul(pd[:], zh, fwml[:, par:par + 512],
                                         start=False, stop=False)
                                mi = T.matmul(pd[:], zl, fwmh[:, par:par + 512],
                                              start=False, stop=(k == 63))
                            sPE.inc(mi)
                            pe_hist[k] = sPE.v
                            if k + 2 < 64:
                                SY.wait_ge(sPE.h, pe_hist[k])
                                _pref_kk(k + 2)
                        V.wait_ge(sPE.h, sPE.v)
                        for lm in range(NLF):
                            pd = fpo0 if lm == 0 else fpo1
                            sl = slice(lm * D + dn * 512,
                                       lm * D + (dn + 1) * 512)
                            sV.inc(V.tensor_tensor(out=fxt[:, sl],
                                                   in0=fxt[:, sl], in1=pd[:],
                                                   op=OP.add))
                        T.wait_ge(sV.h, sV.v)
                    G.wait_ge(sV.h, sV.v)
                    dst = out if to_out else Xout_s
                    for j in range(NLF):
                        sG.inc(G.dma_start(
                            out=dst[c * LCF + j * 128:c * LCF + (j + 1) * 128,
                                    :],
                            in_=fxt[:, j * D:(j + 1) * D]), 16)
                    barrier()
                barrier()

        ffn_stage(X1s, X2s, to_out=(stop_stage == 2))
        if stop_stage == 2:
            G.wait_ge(sG.h, sG.v)
            ctx.close()
            return nc

        with contextlib.ExitStack() as es_b:
            bxt = es_b.enter_context(nc.sbuf_tensor("b_xt", [128, 2 * D], F32))
            bstat = es_b.enter_context(nc.sbuf_tensor("b_stat", [128, 8], F32))
            blnh = es_b.enter_context(nc.sbuf_tensor("b_lnh", [128, 16 * 256], BF16))
            blnl = es_b.enter_context(nc.sbuf_tensor("b_lnl", [128, 16 * 256], BF16))
            bK = es_b.enter_context(nc.sbuf_tensor("b_K", [64, L], F32))
            bV = es_b.enter_context(nc.sbuf_tensor("b_V", [64, L], F32))
            bQ = es_b.enter_context(nc.sbuf_tensor("b_Q", [64, L], F32))
            bA = es_b.enter_context(nc.sbuf_tensor("b_A", [64, L], F32))
            bGK = es_b.enter_context(nc.sbuf_tensor("b_GK", [64, L], F32))
            bsqf = es_b.enter_context(nc.sbuf_tensor("b_sqf", [64, L], F32))
            brow = es_b.enter_context(nc.sbuf_tensor("b_row", [1, L], F32))
            bVr = es_b.enter_context(
                nc.sbuf_tensor("b_Vr", [128, 64 * DC], F32))
            bGKr = es_b.enter_context(
                nc.sbuf_tensor("b_GKr", [128, 64 * DC], F32))
            bYr = es_b.enter_context(
                nc.sbuf_tensor("b_Yr", [128, 64 * DC], F32))
            bH = es_b.enter_context(nc.sbuf_tensor("b_H", [64, 64], F32))
            bHd = es_b.enter_context(nc.sbuf_tensor("b_Hd", [64, 64], F32))
            bwe = es_b.enter_context(nc.sbuf_tensor("b_we", [1, 128], F32))
            bsq = es_b.enter_context(nc.sbuf_tensor("b_sq", [1, 64], F32))
            berr = es_b.enter_context(nc.sbuf_tensor("b_err", [1, 1], F32))
            bs = es_b.enter_context(nc.sbuf_tensor("b_s", [1, 1], F32))
            bu = es_b.enter_context(nc.sbuf_tensor("b_u", [1, 64], F32))
            bYt = es_b.enter_context(nc.sbuf_tensor("b_yt", [128, 16 * DC], F32))
            bynh = es_b.enter_context(nc.sbuf_tensor("b_ynh", [64, L], BF16))
            bynl = es_b.enter_context(nc.sbuf_tensor("b_ynl", [64, L], BF16))
            bp0 = es_b.enter_context(nc.psum_tensor("b_p0", [128, 512], F32))
            bpk1 = es_b.enter_context(nc.psum_tensor("b_pk1", [1, 64], F32))
            bpk2 = es_b.enter_context(nc.psum_tensor("b_pk2", [1, 64], F32))
            bpO = es_b.enter_context(nc.psum_tensor("b_pO", [64, 64], F32))
            bpY = es_b.enter_context(nc.psum_tensor("b_pY", [1, 64], F32))
            bpt = es_b.enter_context(nc.psum_tensor("b_pt", [128, 128], F32))
            sWkT = [es_b.enter_context(
                nc.sbuf_tensor(f"zWk{hl}", [128, 16 * DC], BF16))
                for hl in "hl"]
            sWvT = [es_b.enter_context(
                nc.sbuf_tensor(f"zWv{hl}", [128, 16 * DC], BF16))
                for hl in "hl"]
            sWqT = [es_b.enter_context(
                nc.sbuf_tensor(f"zWq{hl}", [128, 16 * DC], BF16))
                for hl in "hl"]
            sWaT = [es_b.enter_context(
                nc.sbuf_tensor(f"zWa{hl}", [128, 16 * DC], BF16))
                for hl in "hl"]
            sWoT = [es_b.enter_context(
                nc.sbuf_tensor(f"zWo{hl}", [64, D], BF16)) for hl in "hl"]
            for sdst, ssrc in ((sWkT, WbT["WkT"]), (sWvT, WbT["WvT"]),
                               (sWqT, WbT["WqT"]), (sWaT, WbT["WaT"])):
                for hl in range(2):
                    sW.inc(SY.dma_start(
                        out=sdst[hl].ap().rearrange("p (k o) -> p k o", o=DC),
                        in_=ssrc[hl].rearrange("(k p) o -> p k o", p=128)), 16)
            for hl in range(2):
                sW.inc(SY.dma_start(out=sWoT[hl][:], in_=WoTs[hl]), 16)
            barrier()
            for c in range(L // 256):
                SY.wait_ge(sV.h, sV.v)
                for j in range(2):
                    sW.inc(SY.dma_start(
                        out=bxt[:, j * D:(j + 1) * D],
                        in_=X2s[c * 256 + j * 128:c * 256 + (j + 1) * 128, :]),
                        16)
                V.wait_ge(sW.h, sW.v)
                S.wait_ge(sW.h, sW.v)
                ln_fm(bxt, blnh, blnl, bstat, bpt, 2, 256)
                T.wait_ge(sV.h, sV.v)
                T.wait_ge(sAC.h, sAC.v)
                for dst, wst in ((bK, sWkT), (bV, sWvT), (bQ, sWqT),
                                 (bA, sWaT)):
                    for k in range(16):
                        mi = mm3(bp0[0:64, 0:256],
                                 wst[0][:, k * DC:(k + 1) * DC],
                                 wst[1][:, k * DC:(k + 1) * DC],
                                 blnh[:, k * 256:(k + 1) * 256],
                                 blnl[:, k * 256:(k + 1) * 256], k == 0, k == 15)
                    sPE.inc(mi)
                    V.wait_ge(sPE.h, sPE.v)
                    sV.inc(V.tensor_copy(dst[:, c * 256:(c + 1) * 256],
                                         bp0[0:64, 0:256]))
                    T.wait_ge(sV.h, sV.v)
            barrier()
            NC5 = min(512, L)
            for dst in (bK, bQ):
                sV.inc(V.tensor_tensor(out=bsqf[:], in0=dst[:], in1=dst[:],
                                       op=OP.mult))
                T.wait_ge(sV.h, sV.v)
                for cc in range(L // NC5):
                    sPE.inc(T.matmul(bp0[0:1, 0:NC5], sones[0:64, 0:1],
                                     bsqf[:, cc * NC5:(cc + 1) * NC5],
                                     start=True, stop=True))
                    S.wait_ge(sPE.h, sPE.v)
                    sAC.inc(S.activation(out=brow[0:1, cc * NC5:(cc + 1) * NC5],
                                         in_=bp0[0:1, 0:NC5], func=AF.Sqrt))
                    T.wait_ge(sAC.h, sAC.v)
                V.wait_ge(sAC.h, sAC.v)
                VS(V.tensor_scalar(out=brow[:], in0=brow[:], scalar1=1e-12,
                                   scalar2=None, op0=OP.max))
                sV.inc(V.reciprocal(out=brow[:], in_=brow[:]))
                T.wait_ge(sV.h, sV.v)
                for cc in range(L // NC5):
                    sPE.inc(T.matmul(bp0[0:64, 0:NC5], sones[0:1, 0:64],
                                     brow[0:1, cc * NC5:(cc + 1) * NC5],
                                     start=True, stop=True))
                    V.wait_ge(sPE.h, sPE.v)
                    sV.inc(V.tensor_tensor(out=dst[:, cc * NC5:(cc + 1) * NC5],
                                           in0=dst[:, cc * NC5:(cc + 1) * NC5],
                                           in1=bp0[0:64, 0:NC5], op=OP.mult))
                    T.wait_ge(sV.h, sV.v)
            S.wait_ge(sV.h, sV.v)
            SS(S.activation(out=sbsm[:, 3:4], in_=sbsm[:, 0:1],
                            func=AF.Sigmoid))
            sAC.inc(S.activation(out=sbsm[:, 4:5], in_=sbsm[:, 3:4], func=AF.Ln,
                                 bias=1e-8))
            sAC.inc(S.activation(out=bA[:], in_=bA[:], func=AF.Sigmoid,
                                 bias=sbsm[:, 1:2]))
            V.wait_ge(sAC.h, sAC.v)
            sV.inc(V.tensor_scalar(out=bA[:], in0=bA[:], scalar1=sbsm[:, 4:5],
                                   scalar2=None, op0=OP.mult))
            S.wait_ge(sV.h, sV.v)
            sAC.inc(S.activation(out=bA[:], in_=bA[:], func=AF.Exp, scale=8.0))
            V.wait_ge(sAC.h, sAC.v)
            V.tensor_scalar(out=bGK[:], in0=bA[:], scalar1=-1.0, scalar2=1.0,
                            op0=OP.mult, op1=OP.add)
            sV.inc(V.tensor_tensor(out=bGK[:], in0=bGK[:], in1=bK[:],
                                   op=OP.mult))
            barrier()
            if dbg:
                G.wait_ge(sV.h, sV.v)
                G.wait_ge(sAC.h, sAC.v)
                for i2, srcb in enumerate((bK, bQ, bV, bA, bGK)):
                    sG.inc(G.dma_start(out=d_bk[i2 * 64:(i2 + 1) * 64, :],
                                       in_=srcb[:]), 16)
                barrier()
            V.memset(bH[:], 0.0)
            sV.inc(V.memset(bYt[:], 0.0))
            barrier()
            SIGSC = float(1.0 / (1.0 + 1e-6))
            for cb in range(L // 256):
                for lt2 in range(2):
                    lt = cb * 2 + lt2
                    for srcf, rdst in ((bV, bVr), (bGK, bGKr)):
                        T.wait_ge(sV.h, sV.v)
                        T.wait_ge(sG.h, sG.v)
                        for g in range(4):
                            sPE.inc(T.transpose(
                                bpt[0:32, 0:64],
                                srcf[:, lt * 128 + g:(lt + 1) * 128:4],
                                sid[0:64, 0:64]))
                            V.wait_ge(sPE.h, sPE.v)
                            V.wait_ge(sG.h, sG.v)
                            sV.inc(V.tensor_copy(
                                bYt[32 * g:32 * g + 32, 0:DC],
                                bpt[0:32, 0:64]))
                            T.wait_ge(sV.h, sV.v)
                        G.wait_ge(sV.h, sV.v)
                        for g in range(4):
                            sG.inc(G.dma_start(
                                out=rdst[32 * g:32 * g + 1,
                                         lt2 * 32 * DC:(lt2 + 1) * 32 * DC],
                                in_=bYt[32 * g:32 * g + 32, 0:DC]), 16)
                barrier()
                for tt in range(256):
                    t = cb * 256 + tt
                    g = 32 * (tt % 4)
                    qo = (tt // 4) * DC
                    T.wait_ge(sV.h, sV.v)
                    T.matmul(bpk1[:], bQ[:, t:t + 1], bH[:], start=True,
                             stop=True)
                    sPE.inc(T.matmul(bpk2[:], bK[:, t:t + 1], bH[:],
                                     start=True, stop=True))
                    V.wait_ge(sPE.h, sPE.v)
                    V.tensor_tensor(out=bwe[0:1, 0:64],
                                    in0=bVr[g:g + 1, qo:qo + DC], in1=bpk1[:],
                                    op=OP.subtract)
                    VS(V.tensor_tensor(out=bwe[0:1, 64:128],
                                       in0=bVr[g:g + 1, qo:qo + DC],
                                       in1=bpk2[:], op=OP.subtract))
                    VS(V.tensor_tensor(out=bsq[:], in0=bwe[0:1, 0:64],
                                       in1=bwe[0:1, 0:64], op=OP.mult))
                    V.tensor_scalar(out=bHd[:], in0=bH[:],
                                    scalar1=bA[:, t:t + 1],
                                    scalar2=None, op0=OP.mult)
                    sV.inc(V.reduce_sum(out=berr[:], in_=bsq[:], axis=AX.X))
                    S.wait_ge(sV.h, sV.v)
                    sAC.inc(S.activation(out=bs[:], in_=berr[:],
                                         func=AF.Sigmoid, scale=SIGSC))
                    V.wait_ge(sAC.h, sAC.v)
                    sV.inc(V.tensor_scalar(out=bu[:],
                                           in0=bGKr[g:g + 1, qo:qo + DC],
                                           scalar1=bs[0:1, 0:1], scalar2=None,
                                           op0=OP.mult))
                    T.wait_ge(sV.h, sV.v)
                    sPE.inc(T.matmul(bpO[:], bu[:], bwe[0:1, 64:128],
                                     start=True, stop=True))
                    V.wait_ge(sPE.h, sPE.v)
                    sV.inc(V.tensor_tensor(out=bH[:], in0=bHd[:], in1=bpO[:],
                                           op=OP.add))
                    T.wait_ge(sV.h, sV.v)
                    sPE.inc(T.matmul(bpY[:], bQ[:, t:t + 1], bH[:], start=True,
                                     stop=True))
                    V.wait_ge(sPE.h, sPE.v)
                    sV.inc(V.tensor_copy(bYr[g:g + 1, qo:qo + DC], bpY[:]))
                barrier()
                for lt2 in range(2):
                    lt = cb * 2 + lt2
                    G.wait_ge(sV.h, sV.v)
                    for g in range(4):
                        sG.inc(G.dma_start(
                            out=bYt[32 * g:32 * g + 32, 0:DC],
                            in_=bYr[32 * g:32 * g + 1,
                                    lt2 * 32 * DC:(lt2 + 1) * 32 * DC]), 16)
                    V.wait_ge(sG.h, sG.v)
                    S.wait_ge(sG.h, sG.v)
                    sAC.inc(S.activation(out=ln_tmp[:, 0:DC], in_=bYt[:, 0:DC],
                                         func=AF.Square,
                                         accum_out=bstat[:, 0:1]))
                    V.wait_ge(sAC.h, sAC.v)
                    sV.inc(V.tensor_scalar(out=bstat[:, 0:1],
                                           in0=bstat[:, 0:1],
                                           scalar1=1.0 / DC, scalar2=1e-6,
                                           op0=OP.mult, op1=OP.add))
                    S.wait_ge(sV.h, sV.v)
                    sAC.inc(S.activation(out=bstat[:, 1:2], in_=bstat[:, 0:1],
                                         func=AF.Sqrt))
                    V.wait_ge(sAC.h, sAC.v)
                    VS(V.reciprocal(out=bstat[:, 1:2], in_=bstat[:, 1:2]))
                    sV.inc(V.tensor_scalar(out=bYt[:, 0:DC], in0=bYt[:, 0:DC],
                                           scalar1=bstat[:, 1:2], scalar2=None,
                                           op0=OP.mult))
                    T.wait_ge(sV.h, sV.v)
                    sPE.inc(T.transpose(bpt[0:64, :], bYt[:, 0:DC], sid[:]))
                    V.wait_ge(sPE.h, sPE.v)
                    V.tensor_scalar(
                        out=bsqf[:, lt * 128:(lt + 1) * 128].rearrange(
                            "c (q four) -> c four q", four=4),
                        in0=bpt[0:64, :], scalar1=sbsm[:, 2:3],
                        scalar2=None, op0=OP.mult)
                    V.tensor_copy(bynh[:, lt * 128:(lt + 1) * 128],
                                  bsqf[:, lt * 128:(lt + 1) * 128])
                    sV.inc(V.tensor_tensor(
                        out=bynl[:, lt * 128:(lt + 1) * 128],
                        in0=bsqf[:, lt * 128:(lt + 1) * 128],
                        in1=bynh[:, lt * 128:(lt + 1) * 128],
                        op=OP.subtract))
                    T.wait_ge(sV.h, sV.v)
                barrier()
            barrier()
            if dbg:
                sG.inc(G.dma_start(out=d_yn, in_=bsqf[:]), 16)
                barrier()
            for lt in range(L // 128):
                SY.wait_ge(sV.h, sV.v)
                SY.wait_ge(sG.h, sG.v)
                sW.inc(SY.dma_start(out=bxt[:, 0:D],
                                    in_=X2s[lt * 128:(lt + 1) * 128, :]), 16)
                V.wait_ge(sW.h, sW.v)
                for dn in range(4):
                    T.matmul(bp0[:], bynh[:, lt * 128:(lt + 1) * 128],
                             sWoT[0][:, dn * 512:(dn + 1) * 512], start=True,
                             stop=False)
                    T.matmul(bp0[:], bynl[:, lt * 128:(lt + 1) * 128],
                             sWoT[0][:, dn * 512:(dn + 1) * 512], start=False,
                             stop=False)
                    sPE.inc(T.matmul(bp0[:], bynh[:, lt * 128:(lt + 1) * 128],
                                     sWoT[1][:, dn * 512:(dn + 1) * 512],
                                     start=False, stop=True))
                    V.wait_ge(sPE.h, sPE.v)
                    sV.inc(V.tensor_tensor(
                        out=bxt[:, dn * 512:(dn + 1) * 512],
                        in0=bxt[:, dn * 512:(dn + 1) * 512], in1=bp0[:],
                        op=OP.add))
                    T.wait_ge(sV.h, sV.v)
                G.wait_ge(sV.h, sV.v)
                dst = out if stop_stage == 3 else X3s
                sG.inc(G.dma_start(out=dst[lt * 128:(lt + 1) * 128, :],
                                   in_=bxt[:, 0:D]), 16)
            barrier()
        barrier()
        if stop_stage == 3:
            G.wait_ge(sG.h, sG.v)
            ctx.close()
            return nc

        ffn_stage(X3s, None, to_out=True)
        G.wait_ge(sG.h, sG.v)
        ctx.close()
        return nc


def _hadamard(n):
    m = np.array([[1.0]], dtype=np.float64)
    while m.shape[0] < n:
        m = np.kron(np.array([[1.0, 1.0], [1.0, -1.0]]), m)
    return m


def kernel(**inputs):
    import ml_dtypes
    L = 2048
    nc = build(L=L)
    consts = {
        "c_ident": np.eye(128, dtype=np.float32),
        "c_h64": np.vstack([_hadamard(64), _hadamard(64)]).astype(ml_dtypes.bfloat16),
        "c_h128": _hadamard(128).astype(ml_dtypes.bfloat16),
        "c_ones": np.ones((128, 128), dtype=np.float32),
    }
    names = {
        "a_W_in": "L0_attn_a_W_in", "a_conv_w": "L0_attn_a_conv_w",
        "a_conv_b": "L0_attn_a_conv_b", "a_W_x": "L0_attn_a_W_x",
        "a_W_dt": "L0_attn_a_W_dt", "a_dt_bias": "L0_attn_a_dt_bias",
        "a_A": "L0_attn_a_A", "a_D": "L0_attn_a_D",
        "a_W_out": "L0_attn_a_W_out",
        "b_Wk": "L0_attn_b_Wk", "b_Wv": "L0_attn_b_Wv", "b_Wq": "L0_attn_b_Wq",
        "b_Wa_w": "L0_attn_b_Wa_w", "b_Wa_b": "L0_attn_b_Wa_b",
        "b_lam": "L0_attn_b_lam", "b_norm_w": "L0_attn_b_norm_w",
        "b_Wo": "L0_attn_b_Wo",
        "f_W1": "L0_ffn_W1", "f_W2": "L0_ffn_W2", "f_W3": "L0_ffn_W3",
    }
    x = np.ascontiguousarray(inputs["L0_attn_a_hidden_states"])
    base = {k: np.ascontiguousarray(inputs[v]) for k, v in names.items()}
    base.update(consts)
    in_maps = []
    for c in range(8):
        m = dict(base)
        m["x"] = np.ascontiguousarray(x[c])
        in_maps.append(m)
    res = run_bass_kernel_spmd(nc, in_maps, core_ids=list(range(8)))
    return np.stack([r["out"] for r in res.results], axis=0)
